# version 11
# baseline (speedup 1.0000x reference)
import numpy as np
import ml_dtypes
from contextlib import ExitStack

import concourse.bass as bass
import concourse.mybir as mybir
from concourse.bass_core_rust import IndirectOffsetOnAxis
from concourse.bass_utils import run_bass_kernel_spmd

F32 = mybir.dt.float32
BF16 = mybir.dt.bfloat16
I32 = mybir.dt.int32
AF = mybir.ActivationFunctionType
ALU = mybir.AluOpType
AX = mybir.AxisListType.X
BF = ml_dtypes.bfloat16

S = 2048
DM = 4096
NH = 16
HD = 128
NKT = 32
NQT = 16
TOWN = 1024
NOT = TOWN // 128
INV_D = 1.0 / DM
EPS = 1e-5

_NC_CACHE = {}


class Sem:
    def __init__(self, stack, nc, name):
        self.h = stack.enter_context(nc.semaphore(name))
        self.n = 0

    def inc(self, instr, k=1):
        instr.then_inc(self.h, k)
        self.n += k
        return self.n


def build_nc():
    if "nc" in _NC_CACHE:
        return _NC_CACHE["nc"]
    nc = bass.Bass("TRN2", target_bir_lowering=False)

    x_seq = nc.dram_tensor("x_seq", [S, DM], F32, kind="ExternalInput").ap()
    x_own = nc.dram_tensor("x_own", [TOWN, DM], F32, kind="ExternalInput").ap()
    wq_t = nc.dram_tensor("wq_t", [NH, 128, NKT, 128], F32, kind="ExternalInput").ap()
    wk_t = nc.dram_tensor("wk_t", [NH, 128, NKT, 128], F32, kind="ExternalInput").ap()
    wv_t = nc.dram_tensor("wv_t", [8, 128, NKT, 256], F32, kind="ExternalInput").ap()
    wo_t = nc.dram_tensor("wo_t", [16, 128, NKT, 256], F32, kind="ExternalInput").ap()
    wb_t = nc.dram_tensor("wb_t", [16, 128, NKT, 256], F32, kind="ExternalInput").ap()
    war_t = nc.dram_tensor("war_t", [128, NKT, 136], F32, kind="ExternalInput").ap()
    bbd_h = nc.dram_tensor("bbd", [128, DM], F32, kind="ExternalInput").ap()
    bbias_h = nc.dram_tensor("bbias", [128, DM], F32, kind="ExternalInput").ap()
    rcq_h = nc.dram_tensor("rcq", [128, S], BF16, kind="ExternalInput").ap()
    rsq_h = nc.dram_tensor("rsq", [128, S], BF16, kind="ExternalInput").ap()
    rck_h = nc.dram_tensor("rck", [128, S], BF16, kind="ExternalInput").ap()
    rsk_h = nc.dram_tensor("rsk", [128, S], BF16, kind="ExternalInput").ap()
    madd_h = nc.dram_tensor("madd", [128, 512], F32, kind="ExternalInput").ap()
    iden_h = nc.dram_tensor("iden", [128, 128], BF16, kind="ExternalInput").ap()
    cst_h = nc.dram_tensor("cst", [128, 8], F32, kind="ExternalInput").ap()
    gix_h = nc.dram_tensor("gix", [128, 32], I32, kind="ExternalInput").ap()

    y = nc.dram_tensor("y", [TOWN, DM], F32, kind="ExternalOutput").ap()

    qf = nc.dram_tensor("qf", [NH, 128, S], BF16)
    kfd = nc.dram_tensor("kfd", [NH, 128, S], BF16)
    vtd = nc.dram_tensor("vtd", [NH, NQT, 128, 128], BF16)
    od0 = nc.dram_tensor("od0", [S, 1536], BF16)
    od1 = nc.dram_tensor("od1", [S, 512], BF16)
    g0 = nc.dram_tensor("g0", [2 * S, 1536], BF16)
    g1 = nc.dram_tensor("g1", [2 * S, 512], BF16)
    x2d = nc.dram_tensor("x2d", [NOT, 128, DM], F32)
    ffnd = nc.dram_tensor("ffnd", [NOT, 128, DM], BF16)
    atnd = nc.dram_tensor("atnd", [NOT, 128, DM], BF16)

    st = ExitStack()
    nc._kernel_stack = st
    en = st.enter_context

    cst = en(nc.sbuf_tensor("cst_s", [128, 8], F32))
    iden = en(nc.sbuf_tensor("iden_s", [128, 128], BF16))
    madd = en(nc.sbuf_tensor("madd_s", [128, 512], F32))
    gix = en(nc.sbuf_tensor("gix_s", [128, 32], I32))
    spacer = en(nc.sbuf_tensor("spacer", [128, 1], F32))
    spacer2 = en(nc.sbuf_tensor("spacer2", [128, 1], F32))

    def vsp(v):
        v.tensor_scalar(spacer[:, :], cst[:, 3:4], 1.0, None, ALU.mult)

    def asp(sc):
        sc.activation(spacer2[:, :], cst[:, 3:4], AF.Identity)

    sem = lambda name: Sem(st, nc, name)

    s_c0 = sem("c0")
    with nc.Block("p0") as blk:
        @blk.sync
        def _(sy):
            s_c0.inc(sy.dma_start(out=cst[:, :], in_=cst_h), 16)
            s_c0.inc(sy.dma_start(out=iden[:, :], in_=iden_h), 16)
            s_c0.inc(sy.dma_start(out=madd[:, :], in_=madd_h), 16)
            s_c0.inc(sy.dma_start(out=gix[:, :], in_=gix_h), 16)
            sy.wait_ge(s_c0.h, s_c0.n)

    p12 = ExitStack()
    en12 = p12.enter_context
    x1f = en12(nc.sbuf_tensor("x1f", [128, NKT, S], BF16))

    p2a = ExitStack()
    en1 = p2a.enter_context
    en2 = en1
    xsb = [en1(nc.sbuf_tensor(f"xsb{j}", [128, DM], BF16)) for j in range(2)]
    x1t = en1(nc.sbuf_tensor("x1t", [128, DM], BF16))
    sumx = en1(nc.sbuf_tensor("sumx", [128, 1], F32))
    sumq = en1(nc.sbuf_tensor("sumq", [128, 2], F32))
    sumqc = en1(nc.sbuf_tensor("sumqc", [128, 1], F32))
    meanb = en1(nc.sbuf_tensor("meanb", [128, 1], F32))
    varb = en1(nc.sbuf_tensor("varb", [128, 1], F32))
    lnvb = en1(nc.sbuf_tensor("lnvb", [128, 1], F32))
    msq1 = en1(nc.sbuf_tensor("msq1", [128, 1], F32))
    negme1 = en1(nc.sbuf_tensor("negme1", [128, 1], F32))
    rstdb = en1(nc.sbuf_tensor("rstdb", [128, 1], F32))
    negmb = en1(nc.sbuf_tensor("negmb", [128, 1], F32))
    ptp1 = [en1(nc.psum_tensor(f"ptp1{j}", [128, 512], BF16)) for j in range(2)]
    rcq = en2(nc.sbuf_tensor("rcq_s", [128, S], BF16))
    rsq = en2(nc.sbuf_tensor("rsq_s", [128, S], BF16))
    rck = en2(nc.sbuf_tensor("rck_s", [128, S], BF16))
    rsk = en2(nc.sbuf_tensor("rsk_s", [128, S], BF16))
    wqsb = en2(nc.sbuf_tensor("wqsb", [128, NKT, 128], BF16))
    wksb = en2(nc.sbuf_tensor("wksb", [128, NKT, 128], BF16))
    qrow = en2(nc.sbuf_tensor("qrow", [128, S], BF16))
    krow = en2(nc.sbuf_tensor("krow", [128, S], BF16))
    swb = en2(nc.sbuf_tensor("swb", [128, S], BF16))
    t1b = en2(nc.sbuf_tensor("t1b", [128, S], BF16))
    psq = [en2(nc.psum_tensor(f"psq{j}", [128, 512], F32)) for j in range(2)]
    psk = [en2(nc.psum_tensor(f"psk{j}", [128, 512], F32)) for j in range(2)]

    s_x = sem("p1x")
    s_sq = sem("p1sq")
    s_v1 = sem("p1v")
    s_r1 = sem("p1r")
    s_nm1 = sem("p1nm")
    s_n1 = sem("p1n")
    s_tp1 = sem("p1tp")
    s_e1 = sem("p1e")
    s_gw = sem("p2gw")
    s_pch = sem("p2p")
    s_ech = sem("p2e")
    s_rot = sem("p2r")
    s_qkw = sem("p2w")
    s_rt = sem("p2rt")

    with nc.Block("p12") as blk:
        @blk.gpsimd
        def _(g):
            for tt in range(NQT):
                if tt >= 2:
                    g.wait_ge(s_n1.h, tt - 1)
                s_x.inc(g.dma_start(out=xsb[tt % 2][:, :],
                                    in_=x_seq[tt * 128:(tt + 1) * 128, :]), 16)
                if tt == 1:
                    s_gw.inc(g.dma_start(out=wqsb[:, :, :], in_=wq_t[0, :, :, :]), 16)
                if tt == 3:
                    s_gw.inc(g.dma_start(out=wksb[:, :, :], in_=wk_t[0, :, :, :]), 16)
            for h in range(1, NH):
                g.wait_ge(s_pch.h, 8 * (h - 1) + 4)
                s_gw.inc(g.dma_start(out=wqsb[:, :, :], in_=wq_t[h, :, :, :]), 16)
                g.wait_ge(s_pch.h, 8 * h)
                s_gw.inc(g.dma_start(out=wksb[:, :, :], in_=wk_t[h, :, :, :]), 16)
            g.wait_ge(s_x.h, s_x.n)

        @blk.vector
        def _(v):
            for tt in range(NQT):
                v.wait_ge(s_x.h, 16 * (tt + 1))
                v.reduce_sum(sumx[:, :], xsb[tt % 2][:, :], AX)
                v.wait_ge(s_sq.h, tt + 1)
                v.tensor_tensor(sumqc[:, :], sumq[:, 0:1], sumq[:, 1:2], ALU.add)
                v.tensor_scalar(meanb[:, :], sumx[:, :], INV_D, None, ALU.mult)
                v.tensor_scalar(negme1[:, :], sumx[:, :], -INV_D, None, ALU.mult)
                v.tensor_scalar(varb[:, :], sumqc[:, :], INV_D, None, ALU.mult)
                v.tensor_tensor(msq1[:, :], meanb[:, :], meanb[:, :], ALU.mult)
                vsp(v)
                s_v1.inc(v.scalar_tensor_tensor(varb[:, :], msq1[:, :], -1.0,
                                                varb[:, :], ALU.mult, ALU.add))
                v.wait_ge(s_r1.h, tt + 1)
                s_nm1.inc(v.tensor_tensor(negmb[:, :], negme1[:, :], rstdb[:, :],
                                          ALU.mult))
                for gidx in range(8):
                    v.wait_ge(s_tp1.h, tt * 8 + gidx + 1)
                    dst = x1f[:, 4 * gidx:4 * gidx + 4, tt * 128:(tt + 1) * 128]
                    s_e1.inc(v.tensor_copy(dst, ptp1[gidx % 2].ap().rearrange(
                        "p (f t) -> p f t", f=4)))
            v.wait_ge(s_rt.h, 64)
            for h in range(NH):
                for which, row, rc, rs in ((0, qrow, rcq, rsq),
                                           (1, krow, rck, rsk)):
                    v.wait_ge(s_ech.h, 8 * h + 4 * (which + 1))
                    v.tensor_tensor(t1b[:, :], row[:, :], rc[:, :], ALU.mult)
                    v.tensor_copy(swb[0:64, :], row[64:128, :])
                    v.tensor_copy(swb[64:128, :], row[0:64, :])
                    v.tensor_tensor(swb[:, :], swb[:, :], rs[:, :], ALU.mult)
                    s_rot.inc(v.tensor_add(row[:, :], t1b[:, :], swb[:, :]))

        @blk.scalar
        def _(sc):
            for tt in range(NQT):
                sc.wait_ge(s_x.h, 16 * (tt + 1))
                if tt >= 1:
                    sc.wait_ge(s_v1.h, tt)
                sc.activation(swb[:, :], xsb[tt % 2][:, 0:2048], AF.Square,
                              accum_out=sumq[:, 0:1])
                s_sq.inc(sc.activation(swb[:, :], xsb[tt % 2][:, 2048:4096],
                                       AF.Square, accum_out=sumq[:, 1:2]))
                sc.wait_ge(s_v1.h, tt + 1)
                asp(sc)
                asp(sc)
                sc.activation(lnvb[:, :], varb[:, :], AF.Ln, bias=cst[:, 2:3])
                s_r1.inc(sc.activation(rstdb[:, :], lnvb[:, :], AF.Exp,
                                       scale=cst[:, 1:2]))
                sc.wait_ge(s_nm1.h, tt + 1)
                asp(sc)
                asp(sc)
                if tt >= 1:
                    sc.wait_ge(s_tp1.h, 8 * tt)
                s_n1.inc(sc.activation(x1t[:, :], xsb[tt % 2][:, :], AF.Identity,
                                       bias=negmb[:, :], scale=rstdb[:, :]))
            cidx = 0
            for h in range(NH):
                for which, row in ((0, qrow), (1, krow)):
                    if h >= 1:
                        sc.wait_ge(s_rot.h, 2 * (h - 1) + which + 1)
                        sc.wait_ge(s_qkw.h, 16 * (2 * (h - 1) + which + 1))
                    for tc in range(4):
                        sc.wait_ge(s_pch.h, cidx + 1)
                        ps = (psq + psk)[cidx % 4]
                        s_ech.inc(sc.copy(row[:, tc * 512:(tc + 1) * 512], ps[:, :]))
                        cidx += 1

        @blk.tensor
        def _(t):
            def qk_chunk(h, which, tc, cidx):
                if h == 0:
                    t.wait_ge(s_e1.h, 8 * (4 * tc + 4))
                if cidx >= 4:
                    t.wait_ge(s_ech.h, cidx - 3)
                wsb = wqsb if which == 0 else wksb
                pst = (psq + psk)[cidx % 4]
                last = None
                for kt in range(NKT):
                    last = t.matmul(pst[:, :], wsb[:, kt, :],
                                    x1f[:, kt, tc * 512:(tc + 1) * 512],
                                    start=(kt == 0), stop=(kt == NKT - 1))
                s_pch.inc(last)

            for tt in range(NQT):
                t.wait_ge(s_n1.h, tt + 1)
                for gidx in range(8):
                    gg = tt * 8 + gidx
                    if gg >= 2:
                        t.wait_ge(s_e1.h, gg - 1)
                    last = None
                    for j in range(4):
                        f = 4 * gidx + j
                        last = t.transpose(ptp1[gidx % 2][:, j * 128:(j + 1) * 128],
                                           x1t[:, f * 128:(f + 1) * 128],
                                           iden[:, :])
                    s_tp1.inc(last)
                if tt % 4 == 3:
                    tc = tt // 4
                    if tc == 0:
                        t.wait_ge(s_gw.h, 16)
                    qk_chunk(0, 0, tc, tc)
            t.wait_ge(s_gw.h, 32)
            cidx = 4
            for tc in range(4):
                qk_chunk(0, 1, tc, cidx)
                cidx += 1
            for h in range(1, NH):
                t.wait_ge(s_gw.h, 16 * (2 * h + 1))
                for tc in range(4):
                    qk_chunk(h, 0, tc, cidx)
                    cidx += 1
                t.wait_ge(s_gw.h, 16 * (2 * h + 2))
                for tc in range(4):
                    qk_chunk(h, 1, tc, cidx)
                    cidx += 1

        @blk.sync
        def _(sy):
            s_rt.inc(sy.dma_start(out=rcq[:, :], in_=rcq_h), 16)
            s_rt.inc(sy.dma_start(out=rsq[:, :], in_=rsq_h), 16)
            s_rt.inc(sy.dma_start(out=rck[:, :], in_=rck_h), 16)
            s_rt.inc(sy.dma_start(out=rsk[:, :], in_=rsk_h), 16)
            for h in range(NH):
                sy.wait_ge(s_rot.h, 2 * h + 1)
                s_qkw.inc(sy.dma_start(out=qf.ap()[h, :, :], in_=qrow[:, :]), 16)
                sy.wait_ge(s_rot.h, 2 * h + 2)
                s_qkw.inc(sy.dma_start(out=kfd.ap()[h, :, :], in_=krow[:, :]), 16)
            sy.wait_ge(s_qkw.h, s_qkw.n)

    p2a.close()
    p3 = ExitStack()
    en3 = p3.enter_context
    wvsb = en3(nc.sbuf_tensor("wvsb", [128, NKT, 256], BF16))
    vstage = [en3(nc.sbuf_tensor(f"vst{j}", [128, 256], BF16)) for j in range(2)]
    psv = en3(nc.psum_tensor("psv", [128, 256], F32))
    ksb = [en3(nc.sbuf_tensor(f"ksb{j}", [128, S], BF16)) for j in range(2)]
    qsb = [en3(nc.sbuf_tensor(f"qsb{j}", [128, S], BF16)) for j in range(2)]
    vsb = [en3(nc.sbuf_tensor(f"vsb{j}", [128, NQT, 128], BF16)) for j in range(2)]
    probs = [en3(nc.sbuf_tensor(f"probs{j}", [128, S], BF16)) for j in range(2)]
    probsT = [en3(nc.sbuf_tensor(f"probsT{j}", [128, S], BF16)) for j in range(2)]
    sums = [en3(nc.sbuf_tensor(f"sums{j}", [128, 2], F32)) for j in range(2)]
    rsum = [en3(nc.sbuf_tensor(f"rsum{j}", [128, 1], F32)) for j in range(2)]
    ohd = [en3(nc.sbuf_tensor(f"ohd{j}", [128, NQT, 128], BF16)) for j in range(2)]
    psc = [en3(nc.psum_tensor(f"psc{j}", [128, 1024], F32)) for j in range(2)]
    ptr3 = [en3(nc.psum_tensor(f"ptr3{j}", [128, 512], BF16)) for j in range(2)]
    po3 = en3(nc.psum_tensor("po3", [128, 128], F32))

    iters = [(h, qt) for h in range(NH) for qt in range(NQT)]
    NI = len(iters)
    nkf = lambda qt: qt // 2 + 1
    ngr = lambda qt: (qt + 4) // 4
    cumgrp = []
    tot = 0
    for (_h, _qt) in iters:
        tot += ngr(_qt)
        cumgrp.append(tot)
    chunks = []
    eck = []
    _cg = 0
    for (_h, _qt) in iters:
        kwtot = nkf(_qt) * 256
        cl = []
        k0 = 0
        while k0 < kwtot:
            _cg += 1
            cl.append((_cg, k0, min(1024, kwtot - k0)))
            k0 += 1024
        chunks.append(cl)
        eck.append(_cg)

    s_gv = sem("p2gv")
    s_pv = sem("p2pv")
    s_ev = sem("p2ev")
    s_vw = sem("p2vw")
    s_kvq = sem("p3kvq")
    s_qkc = sem("p3qkc")
    s_md3 = sem("p3md")
    s_exc = sem("p3exc")
    s_no3 = sem("p3no")
    s_tp3 = sem("p3tp")
    s_ev3 = sem("p3ev")
    s_av3 = sem("p3av")
    s_oe3 = sem("p3oe")
    s_ow3 = sem("p3ow")
    s_cc = sem("p3cc")

    vsched = [(vc, tt) for vc in range(8) for tt in range(NQT)]
    vemit_before = []
    done = 32
    for _i in range(NI):
        vemit_before.append(min(done, 128))
        done += 1

    def p3_loads(sy, h):
        sy.wait_ge(s_vw.h, 256 * (h // 2 + 1))
        s_kvq.inc(sy.dma_start(out=ksb[h % 2][:, :], in_=kfd.ap()[h, :, :]), 16)
        s_kvq.inc(sy.dma_start(out=qsb[h % 2][:, :], in_=qf.ap()[h, :, :]), 16)
        for part in range(4):
            dst = vsb[h % 2][:, 4 * part:4 * part + 4, :]
            src = vtd.ap()[h, 4 * part:4 * part + 4, :, :].rearrange(
                "k p f -> p k f")
            s_kvq.inc(sy.dma_start(out=dst, in_=src), 16)

    def p3_diag(i):
        nk = nkf(iters[i][1])
        dgoff = (nk - 1) * 256
        ci = dgoff // 1024
        return chunks[i][ci][0], dgoff - ci * 1024

    with nc.Block("p3") as blk:
        def vstore(sy, vidx):
            vc, tt = vsched[vidx]
            sy.wait_ge(s_ev.h, vidx + 1)
            dst = vtd.ap()[2 * vc:2 * vc + 2, tt, :, :].rearrange(
                "h p f -> p h f")
            src = vstage[vidx % 2].ap().rearrange("p (h f) -> p h f", h=2)
            s_vw.inc(sy.dma_start(out=dst, in_=src), 16)

        @blk.sync
        def _(sy):
            for vidx in range(32):
                vstore(sy, vidx)
            p3_loads(sy, 0)
            p3_loads(sy, 1)
            vdone = 32
            for h in range(NH):
                while vdone < min(128, 32 + 16 * (h + 1)):
                    vstore(sy, vdone)
                    vdone += 1
                sy.wait_ge(s_oe3.h, 16 * (h + 1))
                odx = od0 if h < 12 else od1
                col = (h * 128) if h < 12 else ((h - 12) * 128)
                for part in range(4):
                    src = ohd[h % 2][:, 4 * part:4 * part + 4, :]
                    dst = odx.ap()[part * 512:(part + 1) * 512,
                                   col:col + 128].rearrange(
                        "(q p) f -> p q f", p=128)
                    s_ow3.inc(sy.dma_start(out=dst, in_=src), 16)
                if h + 2 < NH:
                    sy.wait_ge(s_av3.h, 16 * (h + 1))
                    p3_loads(sy, h + 2)
            sy.wait_ge(s_ow3.h, s_ow3.n)
            sy.wait_ge(s_vw.h, s_vw.n)

        @blk.gpsimd
        def _(g):
            for vc in range(8):
                if vc >= 1:
                    g.wait_ge(s_pv.h, 16 * vc)
                s_gv.inc(g.dma_start(out=wvsb[:, :, :],
                                     in_=wv_t[vc, :, :, :]), 16)
            g.wait_ge(s_ow3.h, 64 * 12)
            s_cc.inc(g.collective_compute(
                "AllGather", ALU.bypass,
                replica_groups=[[0, 1], [2, 3], [4, 5], [6, 7]],
                ins=[od0.ap().opt()], outs=[g0.ap().opt()]), 1)
            g.wait_ge(s_ow3.h, 64 * 16)
            s_cc.inc(g.collective_compute(
                "AllGather", ALU.bypass,
                replica_groups=[[0, 1], [2, 3], [4, 5], [6, 7]],
                ins=[od1.ap().opt()], outs=[g1.ap().opt()]), 1)

        def pe_tav(t, jj):
            hj, qtj = iters[jj]
            t.wait_ge(s_exc.h, eck[jj])
            if qtj <= 3:
                t.ldweights(iden[:, :])
                t.ldweights(iden[:, :])
            base = cumgrp[jj - 1] if jj > 0 else 0
            for gidx in range(ngr(qtj)):
                gg = base + gidx
                if gg >= 2:
                    t.wait_ge(s_ev3.h, gg - 1)
                last = None
                n_in = min(4, qtj + 1 - 4 * gidx)
                for j in range(n_in):
                    kb = 4 * gidx + j
                    last = t.transpose(ptr3[gg % 2][:, j * 128:(j + 1) * 128],
                                       probs[jj % 2][:, kb * 128:(kb + 1) * 128],
                                       iden[:, :])
                s_tp3.inc(last)
            t.wait_ge(s_ev3.h, cumgrp[jj])
            if qtj <= 3:
                t.ldweights(iden[:, :])
                t.ldweights(iden[:, :])
            if jj >= 1:
                t.wait_ge(s_oe3.h, jj)
            last = None
            for kb in range(qtj + 1):
                last = t.matmul(po3[:, :],
                                probsT[jj % 2][:, kb * 128:(kb + 1) * 128],
                                vsb[hj % 2][:, kb, :],
                                start=(kb == 0), stop=(kb == qtj))
            s_av3.inc(last)

        def vchunk(t, vidx):
            vc, tt = vsched[vidx]
            if tt == 0:
                t.wait_ge(s_gv.h, 16 * (vc + 1))
            if vidx >= 1:
                t.wait_ge(s_ev.h, vidx)
            last = None
            for kt in range(NKT):
                last = t.matmul(psv[:, :],
                                x1f[:, kt, tt * 128:(tt + 1) * 128],
                                wvsb[:, kt, :],
                                start=(kt == 0), stop=(kt == NKT - 1))
            s_pv.inc(last)

        @blk.tensor
        def _(t):
            for vidx in range(32):
                vchunk(t, vidx)
            vdone = 32
            for i, (h, qt) in enumerate(iters):
                if vdone < 128:
                    vchunk(t, vdone)
                    vdone += 1
                if qt == 0:
                    t.wait_ge(s_kvq.h, 96 * (h + 1))
                for (cg, k0, kw) in chunks[i]:
                    if cg > 2:
                        t.wait_ge(s_exc.h, cg - 2)
                    last = None
                    j0 = 0
                    while j0 < kw:
                        w = min(512, kw - j0)
                        last = t.matmul(psc[cg % 2][:, j0:j0 + w],
                                        qsb[h % 2][:, qt * 128:(qt + 1) * 128],
                                        ksb[h % 2][:, k0 + j0:k0 + j0 + w],
                                        start=True, stop=True)
                        j0 += w
                    s_qkc.inc(last)
                if i >= 1:
                    pe_tav(t, i - 1)
            pe_tav(t, NI - 1)

        @blk.vector
        def _(v):
            def evacs(jj):
                hj, qtj = iters[jj]
                if jj >= 2:
                    v.wait_ge(s_av3.h, jj - 1)
                base = cumgrp[jj - 1] if jj > 0 else 0
                for gidx in range(ngr(qtj)):
                    gg = base + gidx
                    v.wait_ge(s_tp3.h, gg + 1)
                    if qtj <= 3:
                        vsp(v)
                        vsp(v)
                    n_in = min(4, qtj + 1 - 4 * gidx)
                    s_ev3.inc(v.tensor_copy(
                        probsT[jj % 2][:, gidx * 512:gidx * 512 + n_in * 128],
                        ptr3[gg % 2][:, 0:n_in * 128]))

            for i, (h, qt) in enumerate(iters):
                dgc, dgo = p3_diag(i)
                v.wait_ge(s_qkc.h, dgc)
                s_md3.inc(v.tensor_tensor(
                    psc[dgc % 2][:, dgo:dgo + 256],
                    psc[dgc % 2][:, dgo:dgo + 256],
                    madd[:, (qt % 2) * 256:(qt % 2) * 256 + 256], ALU.add))
                if i >= 1:
                    evacs(i - 1)
                v.wait_ge(s_exc.h, eck[i])
                if i >= 2:
                    v.wait_ge(s_oe3.h, i - 1)
                if len(chunks[i]) == 2:
                    v.tensor_tensor(sums[i % 2][:, 0:1], sums[i % 2][:, 0:1],
                                    sums[i % 2][:, 1:2], ALU.add)
                    vsp(v)
                vsp(v)
                s_no3.inc(v.reciprocal(rsum[i % 2][:, :], sums[i % 2][:, 0:1]))
            evacs(NI - 1)

        @blk.scalar
        def _(sc):
            def vevac(sc2, vidx):
                sc2.wait_ge(s_pv.h, vidx + 1)
                if vidx >= 2:
                    sc2.wait_ge(s_vw.h, 16 * (vidx - 1))
                s_ev.inc(sc2.copy(vstage[vidx % 2][:, :], psv[:, :]))

            def oevac(jj):
                hj, qtj = iters[jj]
                sc.wait_ge(s_av3.h, jj + 1)
                sc.wait_ge(s_no3.h, jj + 1)
                asp(sc)
                if qtj == 0 and hj >= 2:
                    sc.wait_ge(s_ow3.h, 64 * (hj - 1))
                s_oe3.inc(sc.activation(ohd[hj % 2][:, qtj, :], po3[:, :],
                                        AF.Identity, scale=rsum[jj % 2][:, :]))

            for vidx in range(32):
                vevac(sc, vidx)
            vdone = 32
            for i, (h, qt) in enumerate(iters):
                if vdone < 128:
                    vevac(sc, vdone)
                    vdone += 1
                dgc, _dgo = p3_diag(i)
                first = True
                for ci, (cg, k0, kw) in enumerate(chunks[i]):
                    sc.wait_ge(s_qkc.h, cg)
                    if cg == dgc:
                        sc.wait_ge(s_md3.h, i + 1)
                        if qt <= 3:
                            asp(sc)
                    if first and i >= 2:
                        sc.wait_ge(s_tp3.h, cumgrp[i - 2])
                        sc.wait_ge(s_no3.h, i - 1)
                    first = False
                    s_exc.inc(sc.activation(probs[i % 2][:, k0:k0 + kw],
                                            psc[cg % 2][:, 0:kw], AF.Exp,
                                            bias=cst[:, 0:1],
                                            accum_out=sums[i % 2][:, ci:ci + 1]))
                if i >= 1:
                    oevac(i - 1)
            oevac(NI - 1)

    p3.close()
    p12.close()

    p56 = ExitStack()
    en56 = p56.enter_context
    st3a = en56(nc.sbuf_tensor("st3a", [128, NOT], F32))
    st3b = en56(nc.sbuf_tensor("st3b", [128, NOT], F32))
    warsb = en56(nc.sbuf_tensor("warsb", [128, NKT, 136], BF16))
    ghT = en56(nc.sbuf_tensor("ghT", [128, TOWN], BF16))
    wbsb = [en56(nc.sbuf_tensor(f"wbsb{j}", [128, NKT, 256], BF16)) for j in range(2)]
    bbdsb = en56(nc.sbuf_tensor("bbdsb", [128, DM], BF16))
    bbias = en56(nc.sbuf_tensor("bbias_s", [128, DM], BF16))
    pacc = [en56(nc.sbuf_tensor(f"pacc{j}", [128, 1], F32)) for j in range(2)]
    qacc = [en56(nc.sbuf_tensor(f"qacc{j}", [128, 1], F32)) for j in range(2)]

    p5a = ExitStack()
    en5 = p5a.enter_context
    omine = en5(nc.sbuf_tensor("omine", [128, NKT, TOWN], BF16))
    wosb = [en5(nc.sbuf_tensor(f"wosb{j}", [128, NKT, 256], BF16)) for j in range(2)]
    sqj5 = en5(nc.sbuf_tensor("sqj5", [128, 256], BF16))
    astage = [en5(nc.sbuf_tensor(f"ast{j}", [128, 256], BF16)) for j in range(2)]
    otm = [en5(nc.sbuf_tensor(f"otm{j}", [128, DM], BF16)) for j in range(2)]
    pso = [en5(nc.psum_tensor(f"pso{j}", [128, 256], F32)) for j in range(2)]
    ptr5a = [en5(nc.psum_tensor(f"ptr5a{j}", [128, 512], BF16)) for j in range(2)]

    s_aw = sem("p5aw")
    s_ec5 = sem("p5ec")
    s_gg = sem("p5gg")
    s_tp5a = sem("p5tpa")
    s_ev5a = sem("p5eva")
    s_gwo = sem("p5gw")
    s_sm = sem("p5sm")
    s_p5 = sem("p5p")
    s_e5 = sem("p5e")
    s_q5 = sem("p5q")

    g0_flat = g0.ap().rearrange("r c -> (r c)").rearrange("(r c) -> r c", c=1)
    g1_flat = g1.ap().rearrange("r c -> (r c)").rearrange("(r c) -> r c", c=1)

    with nc.Block("p5a") as blk:
        @blk.gpsimd
        def _(g):
            s_gwo.inc(g.dma_start(out=wosb[0][:, :, :], in_=wo_t[0, :, :, :]), 16)
            s_gwo.inc(g.dma_start(out=wosb[1][:, :, :], in_=wo_t[1, :, :, :]), 16)
            g.wait_ge(s_cc.h, 2)
            for qt in range(NOT):
                if qt >= 2:
                    g.wait_ge(s_tp5a.h, 8 * (qt - 1))
                off = 0
                for part, w in enumerate((1536, 512, 1536, 512)):
                    src = g0_flat if part % 2 == 0 else g1_flat
                    s_gg.inc(g.indirect_dma_start(
                        otm[qt % 2][:, off:off + w], None, src,
                        IndirectOffsetOnAxis(gix[:, 4 * qt + part:4 * qt + part + 1], 0)), 16)
                    off += w
            for oc in range(2, 16):
                g.wait_ge(s_p5.h, 8 * (oc - 1))
                s_gwo.inc(g.dma_start(out=wosb[oc % 2][:, :, :],
                                      in_=wo_t[oc, :, :, :]), 16)
            s_gwo.inc(g.dma_start(out=warsb[:, :, :], in_=war_t), 16)
            g.wait_ge(s_gwo.h, s_gwo.n)

        @blk.tensor
        def _(t):
            for qt in range(NOT):
                t.wait_ge(s_gg.h, 64 * (qt + 1))
                for fbg in range(8):
                    gg5 = qt * 8 + fbg
                    if gg5 >= 2:
                        t.wait_ge(s_ev5a.h, gg5 - 1)
                    last = None
                    for j in range(4):
                        fb = 4 * fbg + j
                        last = t.transpose(ptr5a[gg5 % 2][:, j * 128:(j + 1) * 128],
                                           otm[qt % 2][:, fb * 128:(fb + 1) * 128],
                                           iden[:, :])
                    s_tp5a.inc(last)
            t.wait_ge(s_ev5a.h, 64)
            for oc in range(16):
                t.wait_ge(s_gwo.h, 16 * (oc + 1))
                for qt in range(NOT):
                    vidx = oc * 8 + qt
                    if vidx >= 2:
                        t.wait_ge(s_ec5.h, vidx - 1)
                    last = None
                    for kt in range(NKT):
                        last = t.matmul(pso[vidx % 2][:, :],
                                        omine[:, kt, qt * 128:(qt + 1) * 128],
                                        wosb[oc % 2][:, kt, :],
                                        start=(kt == 0), stop=(kt == NKT - 1))
                    s_p5.inc(last)

        @blk.vector
        def _(v):
            v.memset(st3a[:, :], 0.0)
            s_sm.inc(v.memset(st3b[:, :], 0.0))
            for qt in range(NOT):
                for fbg in range(8):
                    gg5 = qt * 8 + fbg
                    v.wait_ge(s_tp5a.h, gg5 + 1)
                    dst = omine[:, 4 * fbg:4 * fbg + 4, qt * 128:(qt + 1) * 128]
                    s_ev5a.inc(v.tensor_copy(dst, ptr5a[gg5 % 2].ap().rearrange(
                        "p (f t) -> p f t", f=4)))
            for oc in range(16):
                for qt in range(NOT):
                    vidx = oc * 8 + qt
                    v.wait_ge(s_p5.h, vidx + 1)
                    if vidx >= 2:
                        v.wait_ge(s_aw.h, 16 * (vidx - 1))
                        v.wait_ge(s_q5.h, vidx - 1)
                    s_ec5.inc(v.tensor_scalar(astage[vidx % 2][:, :],
                                              pso[vidx % 2][:, :], 1.0, None,
                                              ALU.mult, op1=ALU.add,
                                              accum_out=pacc[vidx % 2][:, :]))
                    vsp(v)
                    v.tensor_add(st3a[:, qt:qt + 1], st3a[:, qt:qt + 1],
                                 pacc[vidx % 2][:, :])
                    v.wait_ge(s_q5.h, vidx + 1)
                    vsp(v)
                    s_e5.inc(v.tensor_add(st3b[:, qt:qt + 1], st3b[:, qt:qt + 1],
                                          qacc[vidx % 2][:, :]))

        @blk.scalar
        def _(sc):
            for oc in range(16):
                for qt in range(NOT):
                    vidx = oc * 8 + qt
                    sc.wait_ge(s_ec5.h, vidx + 1)
                    s_q5.inc(sc.activation(sqj5[:, :], astage[vidx % 2][:, :],
                                           AF.Square, accum_out=qacc[vidx % 2][:, :]))

        @blk.sync
        def _(sy):
            for oc in range(16):
                for qt in range(NOT):
                    vidx = oc * 8 + qt
                    sy.wait_ge(s_ec5.h, vidx + 1)
                    s_aw.inc(sy.dma_start(
                        out=atnd.ap()[qt, :, oc * 256:(oc + 1) * 256],
                        in_=astage[vidx % 2][:, :]), 16)
            sy.wait_ge(s_aw.h, s_aw.n)

    p5a.close()

    p5b_outer = ExitStack()
    en5b = p5b_outer.enter_context
    x3f = en5b(nc.sbuf_tensor("x3f", [128, NKT, TOWN], BF16))

    p5b = ExitStack()
    en5c = p5b.enter_context
    x2sb = en5c(nc.sbuf_tensor("x2sb", [128, DM], F32))
    xot = en5c(nc.sbuf_tensor("xot", [128, DM], F32))
    atnld = [en5c(nc.sbuf_tensor(f"atnld{j}", [128, DM], BF16)) for j in range(2)]
    x3t = [en5c(nc.sbuf_tensor(f"x3t{j}", [128, DM], BF16)) for j in range(2)]
    sqj5b = en5c(nc.sbuf_tensor("sqj5b", [128, DM], BF16))
    sum2 = en5c(nc.sbuf_tensor("sum2", [128, 1], F32))
    sq2 = en5c(nc.sbuf_tensor("sq2", [128, 1], F32))
    mb5 = en5c(nc.sbuf_tensor("mb5", [128, 1], F32))
    vb5 = en5c(nc.sbuf_tensor("vb5", [128, 1], F32))
    lb5 = en5c(nc.sbuf_tensor("lb5", [128, 1], F32))
    rb5 = en5c(nc.sbuf_tensor("rb5", [128, 1], F32))
    nb5 = en5c(nc.sbuf_tensor("nb5", [128, 1], F32))
    msq5 = en5c(nc.sbuf_tensor("msq5", [128, 1], F32))
    nme5 = en5c(nc.sbuf_tensor("nme5", [128, 1], F32))
    vb3a = en5c(nc.sbuf_tensor("vb3a", [128, NOT], F32))
    mb3a = en5c(nc.sbuf_tensor("mb3a", [128, NOT], F32))
    nme3a = en5c(nc.sbuf_tensor("nme3a", [128, NOT], F32))
    msq3a = en5c(nc.sbuf_tensor("msq3a", [128, NOT], F32))
    lb3a = en5c(nc.sbuf_tensor("lb3a", [128, NOT], F32))
    rb3a = en5c(nc.sbuf_tensor("rb3a", [128, NOT], F32))
    nb3a = en5c(nc.sbuf_tensor("nb3a", [128, NOT], F32))
    ptp5 = [en5c(nc.psum_tensor(f"ptp5{j}", [128, 512], BF16)) for j in range(2)]
    hsb = en5c(nc.sbuf_tensor("hsb", [128, 128], BF16))
    ghsb = en5c(nc.sbuf_tensor("ghsb", [128, 128], BF16))
    lg = en5c(nc.sbuf_tensor("lg", [128, 8], F32))
    esb = en5c(nc.sbuf_tensor("esb", [128, 8], F32))
    d1b = en5c(nc.sbuf_tensor("d1b", [128, 8], F32))
    s1b = en5c(nc.sbuf_tensor("s1b", [128, 8], F32))
    l2b = en5c(nc.sbuf_tensor("l2b", [128, 8], F32))
    wgt = en5c(nc.sbuf_tensor("wgt", [128, 8], F32))
    i1b = en5c(nc.sbuf_tensor("i1b", [128, 8], F32))
    m1b = en5c(nc.sbuf_tensor("m1b", [128, 1], F32))
    dnb = en5c(nc.sbuf_tensor("dnb", [128, 1], F32))
    rdnb = en5c(nc.sbuf_tensor("rdnb", [128, 1], F32))
    ph6 = en5c(nc.psum_tensor("ph6", [128, 136], F32))
    ptp6 = en5c(nc.psum_tensor("ptp6", [128, 128], BF16))

    s_xo = sem("p5xo")
    s_al = sem("p5al")
    s_v3f = sem("p5v3")
    s_r3f = sem("p5r3")
    s_x2r = sem("p5x2")
    s_w2 = sem("p5w2")
    s_s2a = sem("p5s2a")
    s_s2b = sem("p5s2b")
    s_v2f = sem("p5v2")
    s_r2f = sem("p5r2")
    s_nm2 = sem("p5nm2")
    s_n2 = sem("p5n2")
    s_tp5 = sem("p5tp")
    s_e55 = sem("p5e5")
    s_gwb = sem("p6gw")
    s_p6a = sem("p6ap")
    s_t2a = sem("p6t2a")
    s_t2b = sem("p6t2b")
    s_t2c = sem("p6t2c")
    s_t2d = sem("p6t2d")
    s_gh = sem("p6gh")
    s_tp6 = sem("p6tp")
    s_ge = sem("p6ge")

    with nc.Block("p5b") as blk:
        @blk.gpsimd
        def _(g):
            s_gwb.inc(g.dma_start(out=bbdsb[:, :], in_=bbd_h), 16)
            s_gwb.inc(g.dma_start(out=bbias[:, :], in_=bbias_h), 16)
            s_gwb.inc(g.dma_start(out=wbsb[0][:, :, :], in_=wb_t[0, :, :, :]), 16)
            s_gwb.inc(g.dma_start(out=wbsb[1][:, :, :], in_=wb_t[1, :, :, :]), 16)
            g.wait_ge(s_gwb.h, s_gwb.n)

        @blk.sync
        def _(sy):
            s_xo.inc(sy.dma_start(out=xot[:, :],
                                  in_=x_own[0:128, :]), 16)
            s_al.inc(sy.dma_start(out=atnld[0][:, :], in_=atnd.ap()[0, :, :]), 16)
            s_al.inc(sy.dma_start(out=atnld[1][:, :], in_=atnd.ap()[1, :, :]), 16)
            for qt in range(NOT):
                sy.wait_ge(s_x2r.h, qt + 1)
                s_w2.inc(sy.dma_start(out=x2d.ap()[qt, :, :], in_=x2sb[:, :]), 16)
                if qt + 1 < NOT:
                    s_xo.inc(sy.dma_start(
                        out=xot[:, :],
                        in_=x_own[(qt + 1) * 128:(qt + 2) * 128, :]), 16)
                if qt + 2 < NOT:
                    s_al.inc(sy.dma_start(out=atnld[qt % 2][:, :],
                                          in_=atnd.ap()[qt + 2, :, :]), 16)
            sy.wait_ge(s_w2.h, s_w2.n)

        def top2_v(v, j):
            v.wait_ge(s_p6a.h, j + 1)
            if j >= 1:
                v.wait_ge(s_t2d.h, j)
            v.tensor_copy(hsb[:, :], ph6[:, 0:128])
            vsp(v)
            v.tensor_copy(lg[:, :], ph6[:, 128:136])
            vsp(v)
            v.reduce_max(m1b[:, :], lg[:, :], AX, negate=True)
            vsp(v)
            s_t2a.inc(v.tensor_scalar(d1b[:, :], lg[:, :], m1b[:, :], None,
                                      ALU.add))
            v.wait_ge(s_t2b.h, j + 1)
            vsp(v)
            vsp(v)
            v.tensor_scalar(i1b[:, :], s1b[:, :], 1.0, 1.0, ALU.add, ALU.min)
            vsp(v)
            v.scalar_tensor_tensor(l2b[:, :], i1b[:, :], -1e30, lg[:, :],
                                   ALU.mult, ALU.add)
            vsp(v)
            v.reduce_max(m1b[:, :], l2b[:, :], AX, negate=True)
            vsp(v)
            s_t2c.inc(v.tensor_scalar(d1b[:, :], lg[:, :], m1b[:, :], None,
                                      ALU.add))
            v.wait_ge(s_t2d.h, j + 1)
            vsp(v)
            vsp(v)
            v.tensor_scalar(i1b[:, :], s1b[:, :], 1.0, 1.0, ALU.add, ALU.min)
            vsp(v)
            v.tensor_tensor(esb[:, :], esb[:, :], i1b[:, :], ALU.mult)
            vsp(v)
            v.reduce_sum(dnb[:, :], esb[:, :], AX)
            vsp(v)
            v.reciprocal(rdnb[:, :], dnb[:, :])
            vsp(v)
            v.tensor_scalar(wgt[:, :], esb[:, :], rdnb[:, :], 2.0,
                            ALU.mult, ALU.mult)
            vsp(v)
            for ex in range(8):
                ins = v.tensor_scalar(ghsb[:, ex * 16:(ex + 1) * 16],
                                      hsb[:, ex * 16:(ex + 1) * 16],
                                      wgt[:, ex:ex + 1], None, ALU.mult)
                if ex == 7:
                    s_gh.inc(ins)
            v.wait_ge(s_tp6.h, j + 1)
            s_ge.inc(v.tensor_copy(ghT[:, j * 128:(j + 1) * 128], ptp6[:, :]))

        @blk.vector
        def _(v):
            v.tensor_scalar(vb3a[:, :], st3b[:, :], INV_D, None, ALU.mult)
            v.tensor_scalar(mb3a[:, :], st3a[:, :], INV_D, None, ALU.mult)
            v.tensor_scalar(nme3a[:, :], st3a[:, :], -INV_D, None, ALU.mult)
            v.tensor_tensor(msq3a[:, :], mb3a[:, :], mb3a[:, :], ALU.mult)
            vsp(v)
            s_v3f.inc(v.scalar_tensor_tensor(vb3a[:, :], msq3a[:, :], -1.0,
                                             vb3a[:, :], ALU.mult, ALU.add))
            v.wait_ge(s_r3f.h, 1)
            vsp(v)
            vsp(v)
            vsp(v)
            v.tensor_tensor(nb3a[:, :], nme3a[:, :], rb3a[:, :], ALU.mult)
            vsp(v)
            for qt in range(NOT):
                v.wait_ge(s_xo.h, 16 * (qt + 1))
                v.wait_ge(s_al.h, 16 * (qt + 1))
                if qt >= 1:
                    v.wait_ge(s_n2.h, qt)
                    v.wait_ge(s_w2.h, 16 * qt)
                v.tensor_scalar(xot[:, :], xot[:, :], nb3a[:, qt:qt + 1],
                                None, ALU.add)
                s_x2r.inc(v.scalar_tensor_tensor(x2sb[:, :], atnld[qt % 2][:, :],
                                                 rb3a[:, qt:qt + 1], xot[:, :],
                                                 ALU.mult, ALU.add))
                s_s2a.inc(v.reduce_sum(sum2[:, :], x2sb[:, :], AX))
                v.wait_ge(s_s2b.h, qt + 1)
                v.tensor_scalar(vb5[:, :], sq2[:, :], INV_D, None, ALU.mult)
                v.tensor_scalar(mb5[:, :], sum2[:, :], INV_D, None, ALU.mult)
                v.tensor_scalar(nme5[:, :], sum2[:, :], -INV_D, None, ALU.mult)
                v.tensor_tensor(msq5[:, :], mb5[:, :], mb5[:, :], ALU.mult)
                vsp(v)
                s_v2f.inc(v.scalar_tensor_tensor(vb5[:, :], msq5[:, :], -1.0,
                                                 vb5[:, :], ALU.mult, ALU.add))
                v.wait_ge(s_r2f.h, qt + 1)
                s_nm2.inc(v.tensor_tensor(nb5[:, :], nme5[:, :], rb5[:, :],
                                          ALU.mult))
                base = qt * 8
                for gidx in range(8):
                    gg = base + gidx
                    v.wait_ge(s_tp5.h, gg + 1)
                    dst = x3f[:, 4 * gidx:4 * gidx + 4, qt * 128:(qt + 1) * 128]
                    s_e55.inc(v.tensor_copy(dst, ptp5[gg % 2].ap().rearrange(
                        "p (f t) -> p f t", f=4)))
                if qt >= 1:
                    top2_v(v, qt - 1)
            top2_v(v, NOT - 1)

        def top2_sc(sc, j):
            sc.wait_ge(s_t2a.h, j + 1)
            asp(sc)
            asp(sc)
            sc.activation(esb[:, :], lg[:, :], AF.Exp, bias=m1b[:, :])
            s_t2b.inc(sc.sign(s1b[:, :], d1b[:, :]))
            sc.wait_ge(s_t2c.h, j + 1)
            asp(sc)
            asp(sc)
            s_t2d.inc(sc.sign(s1b[:, :], d1b[:, :]))

        @blk.scalar
        def _(sc):
            sc.wait_ge(s_v3f.h, 1)
            asp(sc)
            asp(sc)
            asp(sc)
            sc.activation(lb3a[:, :], vb3a[:, :], AF.Ln, bias=cst[:, 2:3])
            s_r3f.inc(sc.activation(rb3a[:, :], lb3a[:, :], AF.Exp,
                                    scale=cst[:, 1:2]))
            for qt in range(NOT):
                sc.wait_ge(s_x2r.h, qt + 1)
                s_s2b.inc(sc.activation(sqj5b[:, :], x2sb[:, :], AF.Square,
                                        accum_out=sq2[:, :]))
                sc.wait_ge(s_v2f.h, qt + 1)
                asp(sc)
                asp(sc)
                sc.activation(lb5[:, :], vb5[:, :], AF.Ln, bias=cst[:, 2:3])
                s_r2f.inc(sc.activation(rb5[:, :], lb5[:, :], AF.Exp,
                                        scale=cst[:, 1:2]))
                sc.wait_ge(s_nm2.h, qt + 1)
                asp(sc)
                asp(sc)
                if qt >= 2:
                    sc.wait_ge(s_tp5.h, 8 * (qt - 1))
                s_n2.inc(sc.activation(x3t[qt % 2][:, :], x2sb[:, :], AF.Identity,
                                       bias=nb5[:, :], scale=rb5[:, :]))
                if qt >= 1:
                    top2_sc(sc, qt - 1)
            top2_sc(sc, NOT - 1)

        @blk.tensor
        def _(t):
            def rout(j):
                t.wait_ge(s_e55.h, 8 * (j + 1))
                if j >= 1:
                    t.wait_ge(s_t2a.h, j)
                last = None
                for kt in range(NKT):
                    last = t.matmul(ph6[:, :], x3f[:, kt, j * 128:(j + 1) * 128],
                                    warsb[:, kt, :],
                                    start=(kt == 0), stop=(kt == NKT - 1))
                s_p6a.inc(last)
                t.wait_ge(s_gh.h, j + 1)
                if j >= 1:
                    t.wait_ge(s_ge.h, j)
                s_tp6.inc(t.transpose(ptp6[:, :], ghsb[:, :], iden[:, :]))

            for qt in range(NOT):
                t.wait_ge(s_n2.h, qt + 1)
                for gidx in range(8):
                    gg = qt * 8 + gidx
                    if gg >= 2:
                        t.wait_ge(s_e55.h, gg - 1)
                    last = None
                    for j in range(4):
                        f = 4 * gidx + j
                        last = t.transpose(ptp5[gg % 2][:, j * 128:(j + 1) * 128],
                                           x3t[qt % 2][:, f * 128:(f + 1) * 128],
                                           iden[:, :])
                    s_tp5.inc(last)
                if qt >= 1:
                    rout(qt - 1)
            rout(NOT - 1)

    p5b.close()

    p6b = ExitStack()
    en6b = p6b.enter_context
    fstage = [en6b(nc.sbuf_tensor(f"fst{j}", [128, 256], BF16)) for j in range(2)]
    sqj6 = en6b(nc.sbuf_tensor("sqj6", [128, 256], BF16))
    st4a = en6b(nc.sbuf_tensor("st4a", [128, NOT], F32))
    st4b = en6b(nc.sbuf_tensor("st4b", [128, NOT], F32))
    psf = [en6b(nc.psum_tensor(f"psf{j}", [128, 256], F32)) for j in range(2)]

    s_p6b = sem("p6bp")
    s_ev6 = sem("p6ev")
    s_e6 = sem("p6be")
    s_q6 = sem("p6bq")
    s_fw = sem("p6fw")
    s_z6 = sem("p6z")

    with nc.Block("p6b") as blk:
        @blk.gpsimd
        def _(g):
            for oc in range(2, 16):
                g.wait_ge(s_p6b.h, 8 * (oc - 1))
                s_gwb.inc(g.dma_start(out=wbsb[oc % 2][:, :, :],
                                      in_=wb_t[oc, :, :, :]), 16)

        @blk.tensor
        def _(t):
            for oc in range(16):
                t.wait_ge(s_gwb.h, 16 * (oc + 3))
                for qt in range(NOT):
                    vidx = oc * 8 + qt
                    if vidx >= 2:
                        t.wait_ge(s_e6.h, vidx - 1)
                        t.wait_ge(s_q6.h, vidx - 1)
                    for kt in range(NKT):
                        t.matmul(psf[vidx % 2][:, :],
                                 x3f[:, kt, qt * 128:(qt + 1) * 128],
                                 wbsb[oc % 2][:, kt, :],
                                 start=(kt == 0), stop=False)
                    last = t.matmul(psf[vidx % 2][:, :],
                                    ghT[:, qt * 128:(qt + 1) * 128],
                                    bbdsb[:, oc * 256:(oc + 1) * 256],
                                    start=False, stop=True)
                    s_p6b.inc(last)

        @blk.vector
        def _(v):
            v.memset(st4a[:, :], 0.0)
            s_z6.inc(v.memset(st4b[:, :], 0.0))
            for oc in range(16):
                for qt in range(NOT):
                    vidx = oc * 8 + qt
                    v.wait_ge(s_p6b.h, vidx + 1)
                    if vidx == 0:
                        v.wait_ge(s_gwb.h, 32)
                    if vidx >= 2:
                        v.wait_ge(s_fw.h, 16 * (vidx - 1))
                        v.wait_ge(s_q6.h, vidx - 1)
                    s_ev6.inc(v.scalar_tensor_tensor(fstage[vidx % 2][:, :],
                                                     psf[vidx % 2][:, :], 1.0,
                                                     bbias[:, oc * 256:(oc + 1) * 256],
                                                     ALU.mult, ALU.add,
                                                     accum_out=pacc[vidx % 2][:, :]))
                    vsp(v)
                    v.tensor_add(st4a[:, qt:qt + 1], st4a[:, qt:qt + 1],
                                 pacc[vidx % 2][:, :])
                    v.wait_ge(s_q6.h, vidx + 1)
                    vsp(v)
                    s_e6.inc(v.tensor_add(st4b[:, qt:qt + 1], st4b[:, qt:qt + 1],
                                          qacc[vidx % 2][:, :]))

        @blk.scalar
        def _(sc):
            for oc in range(16):
                for qt in range(NOT):
                    vidx = oc * 8 + qt
                    sc.wait_ge(s_ev6.h, vidx + 1)
                    if vidx >= 2:
                        sc.wait_ge(s_e6.h, vidx - 1)
                    s_q6.inc(sc.activation(sqj6[:, :], fstage[vidx % 2][:, :],
                                           AF.Square, accum_out=qacc[vidx % 2][:, :]))

        @blk.sync
        def _(sy):
            for oc in range(16):
                for qt in range(NOT):
                    vidx = oc * 8 + qt
                    sy.wait_ge(s_ev6.h, vidx + 1)
                    s_fw.inc(sy.dma_start(
                        out=ffnd.ap()[qt, :, oc * 256:(oc + 1) * 256],
                        in_=fstage[vidx % 2][:, :]), 16)
            sy.wait_ge(s_fw.h, s_fw.n)

    p6b.close()

    p6c = ExitStack()
    en6c = p6c.enter_context
    ffnt = [en6c(nc.sbuf_tensor(f"ffnt{j}", [128, DM], BF16)) for j in range(2)]
    x2t = [en6c(nc.sbuf_tensor(f"x2t{j}", [128, DM], F32)) for j in range(2)]
    ysb = [en6c(nc.sbuf_tensor(f"ysb{j}", [128, DM], F32)) for j in range(2)]
    mb6 = en6c(nc.sbuf_tensor("mb6", [128, NOT], F32))
    vb6 = en6c(nc.sbuf_tensor("vb6", [128, NOT], F32))
    lb6 = en6c(nc.sbuf_tensor("lb6", [128, NOT], F32))
    rb6 = en6c(nc.sbuf_tensor("rb6", [128, NOT], F32))
    nb6 = en6c(nc.sbuf_tensor("nb6", [128, NOT], F32))
    msq6 = en6c(nc.sbuf_tensor("msq6", [128, NOT], F32))
    nme6 = en6c(nc.sbuf_tensor("nme6", [128, NOT], F32))

    s_ld6 = sem("p6ld")
    s_v4 = sem("p6v4")
    s_r4 = sem("p6r4")
    s_fy = sem("p6fy")
    s_yw = sem("p6yw")

    with nc.Block("p6c") as blk:
        @blk.sync
        def _(sy):
            for qt in range(min(2, NOT)):
                s_ld6.inc(sy.dma_start(out=ffnt[qt % 2][:, :], in_=ffnd.ap()[qt, :, :]), 16)
                s_ld6.inc(sy.dma_start(out=x2t[qt % 2][:, :], in_=x2d.ap()[qt, :, :]), 16)
            for qt in range(NOT):
                sy.wait_ge(s_fy.h, qt + 1)
                s_yw.inc(sy.dma_start(out=y[qt * 128:(qt + 1) * 128, :],
                                      in_=ysb[qt % 2][:, :]), 16)
                if qt + 2 < NOT:
                    s_ld6.inc(sy.dma_start(out=ffnt[qt % 2][:, :],
                                           in_=ffnd.ap()[qt + 2, :, :]), 16)
                    s_ld6.inc(sy.dma_start(out=x2t[qt % 2][:, :],
                                           in_=x2d.ap()[qt + 2, :, :]), 16)
            sy.wait_ge(s_yw.h, 16 * NOT)

        @blk.vector
        def _(v):
            v.tensor_scalar(vb6[:, :], st4b[:, :], INV_D, None, ALU.mult)
            v.tensor_scalar(mb6[:, :], st4a[:, :], INV_D, None, ALU.mult)
            v.tensor_scalar(nme6[:, :], st4a[:, :], -INV_D, None, ALU.mult)
            v.tensor_tensor(msq6[:, :], mb6[:, :], mb6[:, :], ALU.mult)
            vsp(v)
            s_v4.inc(v.scalar_tensor_tensor(vb6[:, :], msq6[:, :], -1.0,
                                            vb6[:, :], ALU.mult, ALU.add))
            v.wait_ge(s_r4.h, 1)
            vsp(v)
            vsp(v)
            vsp(v)
            v.tensor_tensor(nb6[:, :], nme6[:, :], rb6[:, :], ALU.mult)
            vsp(v)
            for qt in range(NOT):
                v.wait_ge(s_ld6.h, 32 * (qt + 1))
                if qt >= 2:
                    v.wait_ge(s_yw.h, 16 * (qt - 1))
                v.tensor_scalar(x2t[qt % 2][:, :], x2t[qt % 2][:, :],
                                nb6[:, qt:qt + 1], None, ALU.add)
                s_fy.inc(v.scalar_tensor_tensor(ysb[qt % 2][:, :], ffnt[qt % 2][:, :],
                                                rb6[:, qt:qt + 1], x2t[qt % 2][:, :],
                                                ALU.mult, ALU.add))

        @blk.scalar
        def _(sc):
            sc.wait_ge(s_v4.h, 1)
            asp(sc)
            asp(sc)
            asp(sc)
            sc.activation(lb6[:, :], vb6[:, :], AF.Ln, bias=cst[:, 2:3])
            s_r4.inc(sc.activation(rb6[:, :], lb6[:, :], AF.Exp,
                                   scale=cst[:, 1:2]))

    p6c.close()
    p5b_outer.close()
    p56.close()

    _NC_CACHE["nc"] = nc
    return nc


def _host_inputs(ins):
    x = np.ascontiguousarray(np.asarray(ins["L0_attn_hidden_states"], dtype=np.float32))
    wq = np.asarray(ins["L0_attn_wq"], dtype=np.float32)
    wk = np.asarray(ins["L0_attn_wk"], dtype=np.float32)
    wv = np.asarray(ins["L0_attn_wv"], dtype=np.float32)
    wo = np.asarray(ins["L0_attn_wo"], dtype=np.float32)
    wbase = np.asarray(ins["L0_ffn_W_base"], dtype=np.float32)
    bbase = np.asarray(ins["L0_ffn_b_base"], dtype=np.float32)
    wrout = np.asarray(ins["L0_ffn_W_router"], dtype=np.float32)
    lora_a = np.asarray(ins["L0_ffn_lora_A"], dtype=np.float32)
    lora_b = np.asarray(ins["L0_ffn_lora_B"], dtype=np.float32)

    def tile_w(w, fchunk):
        f = w.shape[1]
        return np.ascontiguousarray(
            w.reshape(NKT, 128, f // fchunk, fchunk).transpose(2, 1, 0, 3))

    wo_t = tile_w(wo, 256)
    wb_t = tile_w(np.ascontiguousarray(wbase.T), 256)
    a2t = lora_a.reshape(128, DM).T
    war = np.concatenate([a2t, wrout.T], axis=1)
    war_t = np.ascontiguousarray(war.reshape(NKT, 128, 136).transpose(1, 0, 2))
    bbd = np.ascontiguousarray(lora_b.transpose(0, 2, 1).reshape(128, DM))
    bbias = np.ascontiguousarray(np.broadcast_to(bbase[None, :], (128, DM)))

    inv = (1.0 / 10000.0 ** (np.arange(0, HD, 2, dtype=np.float64) / HD))
    pos = np.arange(S, dtype=np.float64)
    ang = pos[None, :] * np.tile(inv, 2)[:, None]
    cos = np.cos(ang)
    sin = np.sin(ang)
    sgn = np.ones((HD, 1))
    sgn[:64] = -1.0
    ssin = sin * sgn
    sc = 1.0 / np.sqrt(HD)
    rcq = (cos * sc).astype(BF)
    rsq = (ssin * sc).astype(BF)
    rck = cos.astype(BF)
    rsk = ssin.astype(BF)

    qi = np.arange(128)[:, None]
    kk = np.arange(256)[None, :]
    madd = np.zeros((128, 512), dtype=np.float32)
    madd[:, 0:256] = np.where(kk <= qi, 0.0, -200.0)
    madd[:, 256:512] = np.where(kk <= 128 + qi, 0.0, -200.0)

    iden = np.eye(128, dtype=BF)
    cst = np.zeros((128, 8), dtype=np.float32)
    cst[:, 0] = -12.0
    cst[:, 1] = -0.5
    cst[:, 2] = EPS
    cst[:, 3] = 1.0

    in_maps = []
    for c in range(8):
        b, r = c // 2, c % 2
        p_ = np.arange(128)[:, None]
        gixcols = []
        for qt in range(8):
            tokA = r * TOWN + qt * 128 + p_
            for part, w in enumerate((1536, 512, 1536, 512)):
                row = tokA + (2048 if part >= 2 else 0)
                gixcols.append(row * w)
        gix = np.concatenate(gixcols, axis=1).astype(np.int32)
        in_maps.append({
            "x_seq": x[b],
            "x_own": np.ascontiguousarray(x[b, r * TOWN:(r + 1) * TOWN]),
            "wq_t": tile_w(np.ascontiguousarray(wq[:, r * 2048:(r + 1) * 2048]), 128),
            "wk_t": tile_w(np.ascontiguousarray(wk[:, r * 2048:(r + 1) * 2048]), 128),
            "wv_t": tile_w(np.ascontiguousarray(wv[:, r * 2048:(r + 1) * 2048]), 256),
            "wo_t": wo_t, "wb_t": wb_t, "war_t": war_t,
            "bbd": bbd, "bbias": bbias,
            "rcq": rcq, "rsq": rsq, "rck": rck, "rsk": rsk,
            "madd": madd, "iden": iden, "cst": cst, "gix": gix,
        })
    return in_maps


def kernel(**inputs):
    nc = build_nc()
    in_maps = _host_inputs(inputs)
    res = run_bass_kernel_spmd(nc, in_maps, core_ids=list(range(8)))
    out = np.empty((4, S, DM), dtype=np.float32)
    for c in range(8):
        b, r = c // 2, c % 2
        out[b, r * TOWN:(r + 1) * TOWN] = np.asarray(res.results[c]["y"],
                                                     dtype=np.float32)
    return out


# version 13 (final)
# speedup vs baseline: 1.1229x; 1.1229x over previous
import numpy as np
import ml_dtypes
from contextlib import ExitStack

import concourse.bass as bass
import concourse.mybir as mybir
from concourse.bass_core_rust import IndirectOffsetOnAxis
from concourse.bass_utils import run_bass_kernel_spmd

F32 = mybir.dt.float32
BF16 = mybir.dt.bfloat16
I32 = mybir.dt.int32
AF = mybir.ActivationFunctionType
ALU = mybir.AluOpType
AX = mybir.AxisListType.X
BF = ml_dtypes.bfloat16

S = 2048
DM = 4096
NH = 16
HD = 128
NKT = 32
NQT = 16
TOWN = 1024
NOT = TOWN // 128
INV_D = 1.0 / DM
EPS = 1e-5

_NC_CACHE = {}


class Sem:
    def __init__(self, stack, nc, name):
        self.h = stack.enter_context(nc.semaphore(name))
        self.n = 0

    def inc(self, instr, k=1):
        instr.then_inc(self.h, k)
        self.n += k
        return self.n


def build_nc():
    if "nc" in _NC_CACHE:
        return _NC_CACHE["nc"]
    nc = bass.Bass("TRN2", target_bir_lowering=False)

    x_seq = nc.dram_tensor("x_seq", [S, DM], F32, kind="ExternalInput").ap()
    x_own = nc.dram_tensor("x_own", [TOWN, DM], F32, kind="ExternalInput").ap()
    wq_t = nc.dram_tensor("wq_t", [NH, 128, NKT, 128], F32, kind="ExternalInput").ap()
    wk_t = nc.dram_tensor("wk_t", [NH, 128, NKT, 128], F32, kind="ExternalInput").ap()
    wv_t = nc.dram_tensor("wv_t", [8, 128, NKT, 256], F32, kind="ExternalInput").ap()
    wo_t = nc.dram_tensor("wo_t", [16, 128, NKT, 256], F32, kind="ExternalInput").ap()
    wb_t = nc.dram_tensor("wb_t", [16, 128, NKT, 256], F32, kind="ExternalInput").ap()
    war_t = nc.dram_tensor("war_t", [128, NKT, 136], F32, kind="ExternalInput").ap()
    bbd_h = nc.dram_tensor("bbd", [128, DM], F32, kind="ExternalInput").ap()
    bbias_h = nc.dram_tensor("bbias", [128, DM], F32, kind="ExternalInput").ap()
    rcq_h = nc.dram_tensor("rcq", [128, S], BF16, kind="ExternalInput").ap()
    rsq_h = nc.dram_tensor("rsq", [128, S], BF16, kind="ExternalInput").ap()
    rck_h = nc.dram_tensor("rck", [128, S], BF16, kind="ExternalInput").ap()
    rsk_h = nc.dram_tensor("rsk", [128, S], BF16, kind="ExternalInput").ap()
    madd_h = nc.dram_tensor("madd", [128, 512], F32, kind="ExternalInput").ap()
    iden_h = nc.dram_tensor("iden", [128, 128], BF16, kind="ExternalInput").ap()
    cst_h = nc.dram_tensor("cst", [128, 8], F32, kind="ExternalInput").ap()
    gix_h = nc.dram_tensor("gix", [128, 32], I32, kind="ExternalInput").ap()

    y = nc.dram_tensor("y", [TOWN, DM], F32, kind="ExternalOutput").ap()

    qf = nc.dram_tensor("qf", [NH, 128, S], BF16)
    kfd = nc.dram_tensor("kfd", [NH, 128, S], BF16)
    vtd = nc.dram_tensor("vtd", [NH, NQT, 128, 128], BF16)
    od0 = nc.dram_tensor("od0", [S, 1536], BF16)
    od1 = nc.dram_tensor("od1", [S, 512], BF16)
    g0 = nc.dram_tensor("g0", [2 * S, 1536], BF16)
    g1 = nc.dram_tensor("g1", [2 * S, 512], BF16)
    x2d = nc.dram_tensor("x2d", [NOT, 128, DM], F32)
    ffnd = nc.dram_tensor("ffnd", [NOT, 128, DM], BF16)
    atnd = nc.dram_tensor("atnd", [NOT, 128, DM], BF16)

    st = ExitStack()
    nc._kernel_stack = st
    en = st.enter_context

    cst = en(nc.sbuf_tensor("cst_s", [128, 8], F32))
    iden = en(nc.sbuf_tensor("iden_s", [128, 128], BF16))
    madd = en(nc.sbuf_tensor("madd_s", [128, 512], F32))
    gix = en(nc.sbuf_tensor("gix_s", [128, 32], I32))
    spacer = en(nc.sbuf_tensor("spacer", [128, 1], F32))
    spacer2 = en(nc.sbuf_tensor("spacer2", [128, 1], F32))

    def vsp(v):
        v.tensor_scalar(spacer[:, :], cst[:, 3:4], 1.0, None, ALU.mult)

    def asp(sc):
        sc.activation(spacer2[:, :], cst[:, 3:4], AF.Identity)

    sem = lambda name: Sem(st, nc, name)

    s_c0 = sem("c0")
    with nc.Block("p0") as blk:
        @blk.sync
        def _(sy):
            s_c0.inc(sy.dma_start(out=cst[:, :], in_=cst_h), 16)
            s_c0.inc(sy.dma_start(out=iden[:, :], in_=iden_h), 16)
            s_c0.inc(sy.dma_start(out=madd[:, :], in_=madd_h), 16)
            s_c0.inc(sy.dma_start(out=gix[:, :], in_=gix_h), 16)
            sy.wait_ge(s_c0.h, s_c0.n)

    p12 = ExitStack()
    en12 = p12.enter_context
    x1f = en12(nc.sbuf_tensor("x1f", [128, NKT, S], BF16))

    p2a = ExitStack()
    en1 = p2a.enter_context
    en2 = en1
    xsb = [en1(nc.sbuf_tensor(f"xsb{j}", [128, DM], BF16)) for j in range(2)]
    x1t = en1(nc.sbuf_tensor("x1t", [128, DM], BF16))
    sumx = en1(nc.sbuf_tensor("sumx", [128, 1], F32))
    sumq = en1(nc.sbuf_tensor("sumq", [128, 2], F32))
    sumqc = en1(nc.sbuf_tensor("sumqc", [128, 1], F32))
    meanb = en1(nc.sbuf_tensor("meanb", [128, 1], F32))
    varb = en1(nc.sbuf_tensor("varb", [128, 1], F32))
    lnvb = en1(nc.sbuf_tensor("lnvb", [128, 1], F32))
    msq1 = en1(nc.sbuf_tensor("msq1", [128, 1], F32))
    negme1 = en1(nc.sbuf_tensor("negme1", [128, 1], F32))
    rstdb = en1(nc.sbuf_tensor("rstdb", [128, 1], F32))
    negmb = en1(nc.sbuf_tensor("negmb", [128, 1], F32))
    ptp1 = [en1(nc.psum_tensor(f"ptp1{j}", [128, 512], BF16)) for j in range(2)]
    rcq = en2(nc.sbuf_tensor("rcq_s", [128, S], BF16))
    rsq = en2(nc.sbuf_tensor("rsq_s", [128, S], BF16))
    rck = en2(nc.sbuf_tensor("rck_s", [128, S], BF16))
    rsk = en2(nc.sbuf_tensor("rsk_s", [128, S], BF16))
    wqsb = en2(nc.sbuf_tensor("wqsb", [128, NKT, 128], BF16))
    wksb = en2(nc.sbuf_tensor("wksb", [128, NKT, 128], BF16))
    qrow = en2(nc.sbuf_tensor("qrow", [128, S], BF16))
    krow = en2(nc.sbuf_tensor("krow", [128, S], BF16))
    swb = en2(nc.sbuf_tensor("swb", [128, S], BF16))
    t1b = en2(nc.sbuf_tensor("t1b", [128, S], BF16))
    psq = [en2(nc.psum_tensor(f"psq{j}", [128, 512], F32)) for j in range(2)]
    psk = [en2(nc.psum_tensor(f"psk{j}", [128, 512], F32)) for j in range(2)]

    s_x = sem("p1x")
    s_sq = sem("p1sq")
    s_v1 = sem("p1v")
    s_r1 = sem("p1r")
    s_nm1 = sem("p1nm")
    s_n1 = sem("p1n")
    s_tp1 = sem("p1tp")
    s_e1 = sem("p1e")
    s_gw = sem("p2gw")
    s_pch = sem("p2p")
    s_ech = sem("p2e")
    s_rot = sem("p2r")
    s_qkw = sem("p2w")
    s_rt = sem("p2rt")

    with nc.Block("p12") as blk:
        @blk.gpsimd
        def _(g):
            for tt in range(NQT):
                if tt >= 2:
                    g.wait_ge(s_n1.h, tt - 1)
                s_x.inc(g.dma_start(out=xsb[tt % 2][:, :],
                                    in_=x_seq[tt * 128:(tt + 1) * 128, :]), 16)
                if tt == 1:
                    s_gw.inc(g.dma_start(out=wqsb[:, :, :], in_=wq_t[0, :, :, :]), 16)
                if tt == 3:
                    s_gw.inc(g.dma_start(out=wksb[:, :, :], in_=wk_t[0, :, :, :]), 16)
            for h in range(1, NH):
                g.wait_ge(s_pch.h, 8 * (h - 1) + 4)
                s_gw.inc(g.dma_start(out=wqsb[:, :, :], in_=wq_t[h, :, :, :]), 16)
                g.wait_ge(s_pch.h, 8 * h)
                s_gw.inc(g.dma_start(out=wksb[:, :, :], in_=wk_t[h, :, :, :]), 16)
            g.wait_ge(s_x.h, s_x.n)

        @blk.vector
        def _(v):
            for tt in range(NQT):
                v.wait_ge(s_x.h, 16 * (tt + 1))
                v.reduce_sum(sumx[:, :], xsb[tt % 2][:, :], AX)
                v.wait_ge(s_sq.h, tt + 1)
                v.tensor_tensor(sumqc[:, :], sumq[:, 0:1], sumq[:, 1:2], ALU.add)
                v.tensor_scalar(meanb[:, :], sumx[:, :], INV_D, None, ALU.mult)
                v.tensor_scalar(negme1[:, :], sumx[:, :], -INV_D, None, ALU.mult)
                v.tensor_scalar(varb[:, :], sumqc[:, :], INV_D, None, ALU.mult)
                v.tensor_tensor(msq1[:, :], meanb[:, :], meanb[:, :], ALU.mult)
                vsp(v)
                s_v1.inc(v.scalar_tensor_tensor(varb[:, :], msq1[:, :], -1.0,
                                                varb[:, :], ALU.mult, ALU.add))
                v.wait_ge(s_r1.h, tt + 1)
                s_nm1.inc(v.tensor_tensor(negmb[:, :], negme1[:, :], rstdb[:, :],
                                          ALU.mult))
                for gidx in range(8):
                    v.wait_ge(s_tp1.h, tt * 8 + gidx + 1)
                    dst = x1f[:, 4 * gidx:4 * gidx + 4, tt * 128:(tt + 1) * 128]
                    s_e1.inc(v.tensor_copy(dst, ptp1[gidx % 2].ap().rearrange(
                        "p (f t) -> p f t", f=4)))
            v.wait_ge(s_rt.h, 64)
            for h in range(NH):
                for which, row, rc, rs in ((0, qrow, rcq, rsq),
                                           (1, krow, rck, rsk)):
                    v.wait_ge(s_ech.h, 8 * h + 4 * (which + 1))
                    v.tensor_tensor(t1b[:, :], row[:, :], rc[:, :], ALU.mult)
                    v.tensor_copy(swb[0:64, :], row[64:128, :])
                    v.tensor_copy(swb[64:128, :], row[0:64, :])
                    v.tensor_tensor(swb[:, :], swb[:, :], rs[:, :], ALU.mult)
                    s_rot.inc(v.tensor_add(row[:, :], t1b[:, :], swb[:, :]))

        @blk.scalar
        def _(sc):
            for tt in range(NQT):
                sc.wait_ge(s_x.h, 16 * (tt + 1))
                if tt >= 1:
                    sc.wait_ge(s_v1.h, tt)
                sc.activation(swb[:, :], xsb[tt % 2][:, 0:2048], AF.Square,
                              accum_out=sumq[:, 0:1])
                s_sq.inc(sc.activation(swb[:, :], xsb[tt % 2][:, 2048:4096],
                                       AF.Square, accum_out=sumq[:, 1:2]))
                sc.wait_ge(s_v1.h, tt + 1)
                asp(sc)
                asp(sc)
                sc.activation(lnvb[:, :], varb[:, :], AF.Ln, bias=cst[:, 2:3])
                s_r1.inc(sc.activation(rstdb[:, :], lnvb[:, :], AF.Exp,
                                       scale=cst[:, 1:2]))
                sc.wait_ge(s_nm1.h, tt + 1)
                asp(sc)
                asp(sc)
                if tt >= 1:
                    sc.wait_ge(s_tp1.h, 8 * tt)
                s_n1.inc(sc.activation(x1t[:, :], xsb[tt % 2][:, :], AF.Identity,
                                       bias=negmb[:, :], scale=rstdb[:, :]))
            cidx = 0
            for h in range(NH):
                for which, row in ((0, qrow), (1, krow)):
                    if h >= 1:
                        sc.wait_ge(s_rot.h, 2 * (h - 1) + which + 1)
                        sc.wait_ge(s_qkw.h, 16 * (2 * (h - 1) + which + 1))
                    for tc in range(4):
                        sc.wait_ge(s_pch.h, cidx + 1)
                        ps = (psq + psk)[cidx % 4]
                        s_ech.inc(sc.copy(row[:, tc * 512:(tc + 1) * 512], ps[:, :]))
                        cidx += 1

        @blk.tensor
        def _(t):
            def qk_chunk(h, which, tc, cidx):
                if h == 0:
                    t.wait_ge(s_e1.h, 8 * (4 * tc + 4))
                if cidx >= 4:
                    t.wait_ge(s_ech.h, cidx - 3)
                wsb = wqsb if which == 0 else wksb
                pst = (psq + psk)[cidx % 4]
                last = None
                for kt in range(NKT):
                    last = t.matmul(pst[:, :], wsb[:, kt, :],
                                    x1f[:, kt, tc * 512:(tc + 1) * 512],
                                    start=(kt == 0), stop=(kt == NKT - 1))
                s_pch.inc(last)

            for tt in range(NQT):
                t.wait_ge(s_n1.h, tt + 1)
                for gidx in range(8):
                    gg = tt * 8 + gidx
                    if gg >= 2:
                        t.wait_ge(s_e1.h, gg - 1)
                    last = None
                    for j in range(4):
                        f = 4 * gidx + j
                        last = t.transpose(ptp1[gidx % 2][:, j * 128:(j + 1) * 128],
                                           x1t[:, f * 128:(f + 1) * 128],
                                           iden[:, :])
                    s_tp1.inc(last)
                if tt % 4 == 3:
                    tc = tt // 4
                    if tc == 0:
                        t.wait_ge(s_gw.h, 16)
                    qk_chunk(0, 0, tc, tc)
            t.wait_ge(s_gw.h, 32)
            cidx = 4
            for tc in range(4):
                qk_chunk(0, 1, tc, cidx)
                cidx += 1
            for h in range(1, NH):
                t.wait_ge(s_gw.h, 16 * (2 * h + 1))
                for tc in range(4):
                    qk_chunk(h, 0, tc, cidx)
                    cidx += 1
                t.wait_ge(s_gw.h, 16 * (2 * h + 2))
                for tc in range(4):
                    qk_chunk(h, 1, tc, cidx)
                    cidx += 1

        @blk.sync
        def _(sy):
            s_rt.inc(sy.dma_start(out=rcq[:, :], in_=rcq_h), 16)
            s_rt.inc(sy.dma_start(out=rsq[:, :], in_=rsq_h), 16)
            s_rt.inc(sy.dma_start(out=rck[:, :], in_=rck_h), 16)
            s_rt.inc(sy.dma_start(out=rsk[:, :], in_=rsk_h), 16)
            for h in range(NH):
                sy.wait_ge(s_rot.h, 2 * h + 1)
                s_qkw.inc(sy.dma_start(out=qf.ap()[h, :, :], in_=qrow[:, :]), 16)
                sy.wait_ge(s_rot.h, 2 * h + 2)
                s_qkw.inc(sy.dma_start(out=kfd.ap()[h, :, :], in_=krow[:, :]), 16)
            sy.wait_ge(s_qkw.h, s_qkw.n)

    p2a.close()

    p2b = ExitStack()
    en2b = p2b.enter_context
    wvsb = [en2b(nc.sbuf_tensor(f"wvsb{j}", [128, NKT, 256], BF16)) for j in range(2)]
    vstage = [en2b(nc.sbuf_tensor(f"vst{j}", [128, 256], BF16)) for j in range(2)]
    psv = [en2b(nc.psum_tensor(f"psv{j}", [128, 256], F32)) for j in range(2)]

    s_gv = sem("p2gv")
    s_pv = sem("p2pv")
    s_ev = sem("p2ev")
    s_vw = sem("p2vw")

    with nc.Block("p2b") as blk:
        @blk.gpsimd
        def _(g):
            for vc in range(8):
                if vc >= 2:
                    g.wait_ge(s_pv.h, 16 * (vc - 1))
                s_gv.inc(g.dma_start(out=wvsb[vc % 2][:, :, :], in_=wv_t[vc, :, :, :]), 16)

        @blk.tensor
        def _(t):
            for vc in range(8):
                t.wait_ge(s_gv.h, 16 * (vc + 1))
                for tt in range(NQT):
                    vidx = vc * 16 + tt
                    if vidx >= 2:
                        t.wait_ge(s_ev.h, vidx - 1)
                    last = None
                    for kt in range(NKT):
                        last = t.matmul(psv[vidx % 2][:, :],
                                        x1f[:, kt, tt * 128:(tt + 1) * 128],
                                        wvsb[vc % 2][:, kt, :],
                                        start=(kt == 0), stop=(kt == NKT - 1))
                    s_pv.inc(last)

        @blk.scalar
        def _(sc):
            for vidx in range(128):
                sc.wait_ge(s_pv.h, vidx + 1)
                if vidx >= 2:
                    sc.wait_ge(s_vw.h, 16 * (vidx - 1))
                s_ech2 = sc.copy(vstage[vidx % 2][:, :], psv[vidx % 2][:, :])
                s_ev.inc(s_ech2)

        @blk.sync
        def _(sy):
            for vc in range(8):
                for tt in range(NQT):
                    vidx = vc * 16 + tt
                    sy.wait_ge(s_ev.h, vidx + 1)
                    dst = vtd.ap()[2 * vc:2 * vc + 2, tt, :, :].rearrange(
                        "h p f -> p h f")
                    src = vstage[vidx % 2].ap().rearrange("p (h f) -> p h f", h=2)
                    s_vw.inc(sy.dma_start(out=dst, in_=src), 16)
            sy.wait_ge(s_vw.h, s_vw.n)

    p2b.close()
    p12.close()

    p3 = ExitStack()
    en3 = p3.enter_context
    ksb = [en3(nc.sbuf_tensor(f"ksb{j}", [128, S], BF16)) for j in range(2)]
    qsb = [en3(nc.sbuf_tensor(f"qsb{j}", [128, S], BF16)) for j in range(2)]
    vsb = [en3(nc.sbuf_tensor(f"vsb{j}", [128, NQT, 128], BF16)) for j in range(2)]
    probs = [en3(nc.sbuf_tensor(f"probs{j}", [128, S], BF16)) for j in range(2)]
    probsT = [en3(nc.sbuf_tensor(f"probsT{j}", [128, S], BF16)) for j in range(2)]
    sums = [en3(nc.sbuf_tensor(f"sums{j}", [128, 4], F32)) for j in range(2)]
    rsum = [en3(nc.sbuf_tensor(f"rsum{j}", [128, 1], F32)) for j in range(2)]
    ohd = [en3(nc.sbuf_tensor(f"ohd{j}", [128, NQT, 128], BF16)) for j in range(2)]
    psc = [en3(nc.psum_tensor(f"psc{j}", [128, 1024], F32)) for j in range(2)]
    ptr3 = [en3(nc.psum_tensor(f"ptr3{j}", [128, 512], BF16)) for j in range(2)]
    po3 = [en3(nc.psum_tensor(f"po3{j}", [128, 128], F32)) for j in range(2)]

    iters = [(h, qt) for h in range(NH) for qt in range(NQT)]
    NI = len(iters)
    nkf = lambda qt: qt // 2 + 1
    ngr = lambda qt: (qt + 4) // 4
    cumgrp = []
    tot = 0
    for (_h, _qt) in iters:
        tot += ngr(_qt)
        cumgrp.append(tot)
    chunks = []
    eck = []
    _cg = 0
    for (_h, _qt) in iters:
        kwtot = nkf(_qt) * 256
        cl = []
        k0 = 0
        while k0 < kwtot:
            cl.append([None, k0, min(1024, kwtot - k0)])
            k0 += 1024
        dgo = (nkf(_qt) - 1) * 256
        cl.sort(key=lambda c: 0 if c[1] <= dgo < c[1] + c[2] else 1)
        for c in cl:
            _cg += 1
            c[0] = _cg
        chunks.append([tuple(c) for c in cl])
        eck.append(_cg)

    s_kvq = sem("p3kvq")
    s_qkc = sem("p3qkc")
    s_md3 = sem("p3md")
    s_exc = sem("p3exc")
    s_no3 = sem("p3no")
    s_tp3 = sem("p3tp")
    s_ev3 = sem("p3ev")
    s_av3 = sem("p3av")
    s_oe3 = sem("p3oe")
    s_ow3 = sem("p3ow")
    s_cc = sem("p3cc")

    def p3_loads(sy, h):
        s_kvq.inc(sy.dma_start(out=ksb[h % 2][:, :], in_=kfd.ap()[h, :, :]), 16)
        s_kvq.inc(sy.dma_start(out=qsb[h % 2][:, :], in_=qf.ap()[h, :, :]), 16)
        for part in range(4):
            dst = vsb[h % 2][:, 4 * part:4 * part + 4, :]
            src = vtd.ap()[h, 4 * part:4 * part + 4, :, :].rearrange(
                "k p f -> p k f")
            s_kvq.inc(sy.dma_start(out=dst, in_=src), 16)

    def p3_diag(i):
        nk = nkf(iters[i][1])
        dgoff = (nk - 1) * 256
        cg, k0, kw = chunks[i][0]
        return cg, dgoff - k0

    with nc.Block("p3") as blk:
        @blk.sync
        def _(sy):
            p3_loads(sy, 0)
            p3_loads(sy, 1)
            for h in range(NH):
                sy.wait_ge(s_oe3.h, 16 * (h + 1))
                odx = od0 if h < 12 else od1
                col = (h * 128) if h < 12 else ((h - 12) * 128)
                for part in range(4):
                    src = ohd[h % 2][:, 4 * part:4 * part + 4, :]
                    dst = odx.ap()[part * 512:(part + 1) * 512,
                                   col:col + 128].rearrange(
                        "(q p) f -> p q f", p=128)
                    s_ow3.inc(sy.dma_start(out=dst, in_=src), 16)
                if h + 2 < NH:
                    sy.wait_ge(s_av3.h, 16 * (h + 1))
                    p3_loads(sy, h + 2)
            sy.wait_ge(s_ow3.h, s_ow3.n)

        @blk.gpsimd
        def _(g):
            g.wait_ge(s_ow3.h, 64 * 12)
            s_cc.inc(g.collective_compute(
                "AllGather", ALU.bypass,
                replica_groups=[[0, 1], [2, 3], [4, 5], [6, 7]],
                ins=[od0.ap().opt()], outs=[g0.ap().opt()]), 1)
            g.wait_ge(s_ow3.h, 64 * 16)
            s_cc.inc(g.collective_compute(
                "AllGather", ALU.bypass,
                replica_groups=[[0, 1], [2, 3], [4, 5], [6, 7]],
                ins=[od1.ap().opt()], outs=[g1.ap().opt()]), 1)

        def pe_tav(t, jj):
            hj, qtj = iters[jj]
            t.wait_ge(s_exc.h, eck[jj])
            if qtj <= 3:
                t.ldweights(iden[:, :])
                t.ldweights(iden[:, :])
            base = cumgrp[jj - 1] if jj > 0 else 0
            for gidx in range(ngr(qtj)):
                gg = base + gidx
                if gg >= 2:
                    t.wait_ge(s_ev3.h, gg - 1)
                last = None
                n_in = min(4, qtj + 1 - 4 * gidx)
                for j in range(n_in):
                    kb = 4 * gidx + j
                    last = t.transpose(ptr3[gg % 2][:, j * 128:(j + 1) * 128],
                                       probs[jj % 2][:, kb * 128:(kb + 1) * 128],
                                       iden[:, :])
                s_tp3.inc(last)
            t.wait_ge(s_ev3.h, cumgrp[jj])
            if qtj <= 3:
                t.ldweights(iden[:, :])
                t.ldweights(iden[:, :])
            if jj >= 2:
                t.wait_ge(s_oe3.h, jj - 1)
            last = None
            for kb in range(qtj + 1):
                last = t.matmul(po3[jj % 2][:, :],
                                probsT[jj % 2][:, kb * 128:(kb + 1) * 128],
                                vsb[hj % 2][:, kb, :],
                                start=(kb == 0), stop=(kb == qtj))
            s_av3.inc(last)

        @blk.tensor
        def _(t):
            for i, (h, qt) in enumerate(iters):
                if qt == 0:
                    t.wait_ge(s_kvq.h, 96 * (h + 1))
                for (cg, k0, kw) in chunks[i]:
                    if cg > 2:
                        t.wait_ge(s_exc.h, cg - 2)
                    last = None
                    j0 = 0
                    while j0 < kw:
                        w = min(512, kw - j0)
                        last = t.matmul(psc[cg % 2][:, j0:j0 + w],
                                        qsb[h % 2][:, qt * 128:(qt + 1) * 128],
                                        ksb[h % 2][:, k0 + j0:k0 + j0 + w],
                                        start=True, stop=True)
                        j0 += w
                    s_qkc.inc(last)
                if i >= 1:
                    pe_tav(t, i - 1)
            pe_tav(t, NI - 1)

        @blk.vector
        def _(v):
            def evacs(jj):
                hj, qtj = iters[jj]
                if jj >= 2:
                    v.wait_ge(s_av3.h, jj - 1)
                base = cumgrp[jj - 1] if jj > 0 else 0
                for gidx in range(ngr(qtj)):
                    gg = base + gidx
                    v.wait_ge(s_tp3.h, gg + 1)
                    if qtj <= 3:
                        vsp(v)
                        vsp(v)
                    n_in = min(4, qtj + 1 - 4 * gidx)
                    s_ev3.inc(v.tensor_copy(
                        probsT[jj % 2][:, gidx * 512:gidx * 512 + n_in * 128],
                        ptr3[gg % 2][:, 0:n_in * 128]))

            for i, (h, qt) in enumerate(iters):
                dgc, dgo = p3_diag(i)
                v.wait_ge(s_qkc.h, dgc)
                s_md3.inc(v.tensor_tensor(
                    psc[dgc % 2][:, dgo:dgo + 256],
                    psc[dgc % 2][:, dgo:dgo + 256],
                    madd[:, (qt % 2) * 256:(qt % 2) * 256 + 256], ALU.add))
                if i >= 1:
                    evacs(i - 1)
                v.wait_ge(s_exc.h, eck[i])
                if i >= 2:
                    v.wait_ge(s_oe3.h, i - 1)
                ncols = len(chunks[i])
                for cc in range(1, ncols):
                    v.tensor_tensor(sums[i % 2][:, 0:1], sums[i % 2][:, 0:1],
                                    sums[i % 2][:, cc:cc + 1], ALU.add)
                    vsp(v)
                vsp(v)
                s_no3.inc(v.reciprocal(rsum[i % 2][:, :], sums[i % 2][:, 0:1]))
            evacs(NI - 1)

        @blk.scalar
        def _(sc):
            def oevac(jj):
                hj, qtj = iters[jj]
                sc.wait_ge(s_av3.h, jj + 1)
                sc.wait_ge(s_no3.h, jj + 1)
                asp(sc)
                if qtj == 0 and hj >= 2:
                    sc.wait_ge(s_ow3.h, 64 * (hj - 1))
                s_oe3.inc(sc.activation(ohd[hj % 2][:, qtj, :], po3[jj % 2][:, :],
                                        AF.Identity, scale=rsum[jj % 2][:, :]))

            for i, (h, qt) in enumerate(iters):
                dgc, dgo = p3_diag(i)
                if i >= 2:
                    sc.wait_ge(s_tp3.h, cumgrp[i - 2])
                    sc.wait_ge(s_no3.h, i - 1)
                for ci, (cg, k0, kw) in enumerate(chunks[i]):
                    sc.wait_ge(s_qkc.h, cg)
                    if cg == dgc:
                        sc.wait_ge(s_md3.h, i + 1)
                        if qt <= 3:
                            asp(sc)
                    s_exc.inc(sc.activation(probs[i % 2][:, k0:k0 + kw],
                                            psc[cg % 2][:, 0:kw], AF.Exp,
                                            bias=cst[:, 0:1],
                                            accum_out=sums[i % 2][:, ci:ci + 1]))
                if i >= 1:
                    oevac(i - 1)
            oevac(NI - 1)

    p3.close()

    p56 = ExitStack()
    en56 = p56.enter_context
    st3a = en56(nc.sbuf_tensor("st3a", [128, NOT], F32))
    st3b = en56(nc.sbuf_tensor("st3b", [128, NOT], F32))
    warsb = en56(nc.sbuf_tensor("warsb", [128, NKT, 136], BF16))
    ghT = en56(nc.sbuf_tensor("ghT", [128, TOWN], BF16))
    wbsb = [en56(nc.sbuf_tensor(f"wbsb{j}", [128, NKT, 256], BF16)) for j in range(2)]
    bbdsb = en56(nc.sbuf_tensor("bbdsb", [128, DM], BF16))
    bbias = en56(nc.sbuf_tensor("bbias_s", [128, DM], BF16))
    pacc = [en56(nc.sbuf_tensor(f"pacc{j}", [128, 1], F32)) for j in range(2)]
    qacc = [en56(nc.sbuf_tensor(f"qacc{j}", [128, 1], F32)) for j in range(2)]

    p5a = ExitStack()
    en5 = p5a.enter_context
    omine = en5(nc.sbuf_tensor("omine", [128, NKT, TOWN], BF16))
    wosb = [en5(nc.sbuf_tensor(f"wosb{j}", [128, NKT, 256], BF16)) for j in range(2)]
    sqj5 = en5(nc.sbuf_tensor("sqj5", [128, 256], BF16))
    astage = [en5(nc.sbuf_tensor(f"ast{j}", [128, 256], BF16)) for j in range(2)]
    otm = [en5(nc.sbuf_tensor(f"otm{j}", [128, DM], BF16)) for j in range(2)]
    pso = [en5(nc.psum_tensor(f"pso{j}", [128, 256], F32)) for j in range(3)]
    ptr5a = [en5(nc.psum_tensor(f"ptr5a{j}", [128, 512], BF16)) for j in range(2)]

    s_aw = sem("p5aw")
    s_ec5 = sem("p5ec")
    s_gg = sem("p5gg")
    s_tp5a = sem("p5tpa")
    s_ev5a = sem("p5eva")
    s_gwo = sem("p5gw")
    s_sm = sem("p5sm")
    s_p5 = sem("p5p")
    s_e5 = sem("p5e")
    s_q5 = sem("p5q")

    g0_flat = g0.ap().rearrange("r c -> (r c)").rearrange("(r c) -> r c", c=1)
    g1_flat = g1.ap().rearrange("r c -> (r c)").rearrange("(r c) -> r c", c=1)

    with nc.Block("p5a") as blk:
        @blk.gpsimd
        def _(g):
            s_gwo.inc(g.dma_start(out=wosb[0][:, :, :], in_=wo_t[0, :, :, :]), 16)
            s_gwo.inc(g.dma_start(out=wosb[1][:, :, :], in_=wo_t[1, :, :, :]), 16)
            g.wait_ge(s_cc.h, 2)
            for qt in range(NOT):
                if qt >= 2:
                    g.wait_ge(s_tp5a.h, 8 * (qt - 1))
                off = 0
                for part, w in enumerate((1536, 512, 1536, 512)):
                    src = g0_flat if part % 2 == 0 else g1_flat
                    s_gg.inc(g.indirect_dma_start(
                        otm[qt % 2][:, off:off + w], None, src,
                        IndirectOffsetOnAxis(gix[:, 4 * qt + part:4 * qt + part + 1], 0)), 16)
                    off += w
            for oc in range(2, 16):
                g.wait_ge(s_p5.h, 8 * (oc - 1))
                s_gwo.inc(g.dma_start(out=wosb[oc % 2][:, :, :],
                                      in_=wo_t[oc, :, :, :]), 16)
            s_gwo.inc(g.dma_start(out=warsb[:, :, :], in_=war_t), 16)
            g.wait_ge(s_gwo.h, s_gwo.n)

        @blk.tensor
        def _(t):
            for qt in range(NOT):
                t.wait_ge(s_gg.h, 64 * (qt + 1))
                for fbg in range(8):
                    gg5 = qt * 8 + fbg
                    if gg5 >= 2:
                        t.wait_ge(s_ev5a.h, gg5 - 1)
                    last = None
                    for j in range(4):
                        fb = 4 * fbg + j
                        last = t.transpose(ptr5a[gg5 % 2][:, j * 128:(j + 1) * 128],
                                           otm[qt % 2][:, fb * 128:(fb + 1) * 128],
                                           iden[:, :])
                    s_tp5a.inc(last)
            t.wait_ge(s_ev5a.h, 64)
            for oc in range(16):
                t.wait_ge(s_gwo.h, 16 * (oc + 1))
                for qt in range(NOT):
                    vidx = oc * 8 + qt
                    if vidx >= 3:
                        t.wait_ge(s_ec5.h, vidx - 2)
                    last = None
                    for kt in range(NKT):
                        last = t.matmul(pso[vidx % 3][:, :],
                                        omine[:, kt, qt * 128:(qt + 1) * 128],
                                        wosb[oc % 2][:, kt, :],
                                        start=(kt == 0), stop=(kt == NKT - 1))
                    s_p5.inc(last)

        @blk.vector
        def _(v):
            v.memset(st3a[:, :], 0.0)
            s_sm.inc(v.memset(st3b[:, :], 0.0))
            for qt in range(NOT):
                for fbg in range(8):
                    gg5 = qt * 8 + fbg
                    v.wait_ge(s_tp5a.h, gg5 + 1)
                    dst = omine[:, 4 * fbg:4 * fbg + 4, qt * 128:(qt + 1) * 128]
                    s_ev5a.inc(v.tensor_copy(dst, ptr5a[gg5 % 2].ap().rearrange(
                        "p (f t) -> p f t", f=4)))
            for oc in range(16):
                for qt in range(NOT):
                    vidx = oc * 8 + qt
                    v.wait_ge(s_p5.h, vidx + 1)
                    if vidx >= 2:
                        v.wait_ge(s_aw.h, 16 * (vidx - 1))
                        v.wait_ge(s_q5.h, vidx - 1)
                    s_ec5.inc(v.tensor_scalar(astage[vidx % 2][:, :],
                                              pso[vidx % 3][:, :], 1.0, None,
                                              ALU.mult, op1=ALU.add,
                                              accum_out=pacc[vidx % 2][:, :]))
                    vsp(v)
                    v.tensor_add(st3a[:, qt:qt + 1], st3a[:, qt:qt + 1],
                                 pacc[vidx % 2][:, :])
                    v.wait_ge(s_q5.h, vidx + 1)
                    vsp(v)
                    s_e5.inc(v.tensor_add(st3b[:, qt:qt + 1], st3b[:, qt:qt + 1],
                                          qacc[vidx % 2][:, :]))

        @blk.scalar
        def _(sc):
            for oc in range(16):
                for qt in range(NOT):
                    vidx = oc * 8 + qt
                    sc.wait_ge(s_ec5.h, vidx + 1)
                    s_q5.inc(sc.activation(sqj5[:, :], astage[vidx % 2][:, :],
                                           AF.Square, accum_out=qacc[vidx % 2][:, :]))

        @blk.sync
        def _(sy):
            for oc in range(16):
                for qt in range(NOT):
                    vidx = oc * 8 + qt
                    sy.wait_ge(s_ec5.h, vidx + 1)
                    s_aw.inc(sy.dma_start(
                        out=atnd.ap()[qt, :, oc * 256:(oc + 1) * 256],
                        in_=astage[vidx % 2][:, :]), 16)
            sy.wait_ge(s_aw.h, s_aw.n)

    p5a.close()

    p5b_outer = ExitStack()
    en5b = p5b_outer.enter_context
    x3f = en5b(nc.sbuf_tensor("x3f", [128, NKT, TOWN], BF16))

    p5b = ExitStack()
    en5c = p5b.enter_context
    x2sb = en5c(nc.sbuf_tensor("x2sb", [128, DM], F32))
    xot = en5c(nc.sbuf_tensor("xot", [128, DM], F32))
    atnld = [en5c(nc.sbuf_tensor(f"atnld{j}", [128, DM], BF16)) for j in range(2)]
    x3t = [en5c(nc.sbuf_tensor(f"x3t{j}", [128, DM], BF16)) for j in range(2)]
    sqj5b = en5c(nc.sbuf_tensor("sqj5b", [128, DM], BF16))
    sum2 = en5c(nc.sbuf_tensor("sum2", [128, 1], F32))
    sq2 = en5c(nc.sbuf_tensor("sq2", [128, 1], F32))
    mb5 = en5c(nc.sbuf_tensor("mb5", [128, 1], F32))
    vb5 = en5c(nc.sbuf_tensor("vb5", [128, 1], F32))
    lb5 = en5c(nc.sbuf_tensor("lb5", [128, 1], F32))
    rb5 = en5c(nc.sbuf_tensor("rb5", [128, 1], F32))
    nb5 = en5c(nc.sbuf_tensor("nb5", [128, 1], F32))
    msq5 = en5c(nc.sbuf_tensor("msq5", [128, 1], F32))
    nme5 = en5c(nc.sbuf_tensor("nme5", [128, 1], F32))
    vb3a = en5c(nc.sbuf_tensor("vb3a", [128, NOT], F32))
    mb3a = en5c(nc.sbuf_tensor("mb3a", [128, NOT], F32))
    nme3a = en5c(nc.sbuf_tensor("nme3a", [128, NOT], F32))
    msq3a = en5c(nc.sbuf_tensor("msq3a", [128, NOT], F32))
    lb3a = en5c(nc.sbuf_tensor("lb3a", [128, NOT], F32))
    rb3a = en5c(nc.sbuf_tensor("rb3a", [128, NOT], F32))
    nb3a = en5c(nc.sbuf_tensor("nb3a", [128, NOT], F32))
    ptp5 = [en5c(nc.psum_tensor(f"ptp5{j}", [128, 512], BF16)) for j in range(2)]
    hsb = en5c(nc.sbuf_tensor("hsb", [128, 128], BF16))
    ghsb = en5c(nc.sbuf_tensor("ghsb", [128, 128], BF16))
    lg = en5c(nc.sbuf_tensor("lg", [128, 8], F32))
    esb = en5c(nc.sbuf_tensor("esb", [128, 8], F32))
    d1b = en5c(nc.sbuf_tensor("d1b", [128, 8], F32))
    s1b = en5c(nc.sbuf_tensor("s1b", [128, 8], F32))
    l2b = en5c(nc.sbuf_tensor("l2b", [128, 8], F32))
    wgt = en5c(nc.sbuf_tensor("wgt", [128, 8], F32))
    i1b = en5c(nc.sbuf_tensor("i1b", [128, 8], F32))
    m1b = en5c(nc.sbuf_tensor("m1b", [128, 1], F32))
    dnb = en5c(nc.sbuf_tensor("dnb", [128, 1], F32))
    rdnb = en5c(nc.sbuf_tensor("rdnb", [128, 1], F32))
    ph6 = en5c(nc.psum_tensor("ph6", [128, 136], F32))
    ptp6 = en5c(nc.psum_tensor("ptp6", [128, 128], BF16))

    s_xo = sem("p5xo")
    s_al = sem("p5al")
    s_v3f = sem("p5v3")
    s_r3f = sem("p5r3")
    s_x2r = sem("p5x2")
    s_w2 = sem("p5w2")
    s_s2a = sem("p5s2a")
    s_s2b = sem("p5s2b")
    s_v2f = sem("p5v2")
    s_r2f = sem("p5r2")
    s_nm2 = sem("p5nm2")
    s_n2 = sem("p5n2")
    s_tp5 = sem("p5tp")
    s_e55 = sem("p5e5")
    s_gwb = sem("p6gw")
    s_p6a = sem("p6ap")
    s_t2a = sem("p6t2a")
    s_t2b = sem("p6t2b")
    s_t2c = sem("p6t2c")
    s_t2d = sem("p6t2d")
    s_gh = sem("p6gh")
    s_tp6 = sem("p6tp")
    s_ge = sem("p6ge")

    with nc.Block("p5b") as blk:
        @blk.gpsimd
        def _(g):
            s_gwb.inc(g.dma_start(out=bbdsb[:, :], in_=bbd_h), 16)
            s_gwb.inc(g.dma_start(out=bbias[:, :], in_=bbias_h), 16)
            s_gwb.inc(g.dma_start(out=wbsb[0][:, :, :], in_=wb_t[0, :, :, :]), 16)
            s_gwb.inc(g.dma_start(out=wbsb[1][:, :, :], in_=wb_t[1, :, :, :]), 16)
            g.wait_ge(s_gwb.h, s_gwb.n)

        @blk.sync
        def _(sy):
            s_xo.inc(sy.dma_start(out=xot[:, :],
                                  in_=x_own[0:128, :]), 16)
            s_al.inc(sy.dma_start(out=atnld[0][:, :], in_=atnd.ap()[0, :, :]), 16)
            s_al.inc(sy.dma_start(out=atnld[1][:, :], in_=atnd.ap()[1, :, :]), 16)
            for qt in range(NOT):
                sy.wait_ge(s_x2r.h, qt + 1)
                s_w2.inc(sy.dma_start(out=x2d.ap()[qt, :, :], in_=x2sb[:, :]), 16)
                if qt + 1 < NOT:
                    s_xo.inc(sy.dma_start(
                        out=xot[:, :],
                        in_=x_own[(qt + 1) * 128:(qt + 2) * 128, :]), 16)
                if qt + 2 < NOT:
                    s_al.inc(sy.dma_start(out=atnld[qt % 2][:, :],
                                          in_=atnd.ap()[qt + 2, :, :]), 16)
            sy.wait_ge(s_w2.h, s_w2.n)

        def top2_v(v, j):
            v.wait_ge(s_p6a.h, j + 1)
            if j >= 1:
                v.wait_ge(s_t2d.h, j)
            v.tensor_copy(hsb[:, :], ph6[:, 0:128])
            vsp(v)
            v.tensor_copy(lg[:, :], ph6[:, 128:136])
            vsp(v)
            v.reduce_max(m1b[:, :], lg[:, :], AX, negate=True)
            vsp(v)
            s_t2a.inc(v.tensor_scalar(d1b[:, :], lg[:, :], m1b[:, :], None,
                                      ALU.add))
            v.wait_ge(s_t2b.h, j + 1)
            vsp(v)
            vsp(v)
            v.tensor_scalar(i1b[:, :], s1b[:, :], 1.0, 1.0, ALU.add, ALU.min)
            vsp(v)
            v.scalar_tensor_tensor(l2b[:, :], i1b[:, :], -1e30, lg[:, :],
                                   ALU.mult, ALU.add)
            vsp(v)
            v.reduce_max(m1b[:, :], l2b[:, :], AX, negate=True)
            vsp(v)
            s_t2c.inc(v.tensor_scalar(d1b[:, :], lg[:, :], m1b[:, :], None,
                                      ALU.add))
            v.wait_ge(s_t2d.h, j + 1)
            vsp(v)
            vsp(v)
            v.tensor_scalar(i1b[:, :], s1b[:, :], 1.0, 1.0, ALU.add, ALU.min)
            vsp(v)
            v.tensor_tensor(esb[:, :], esb[:, :], i1b[:, :], ALU.mult)
            vsp(v)
            v.reduce_sum(dnb[:, :], esb[:, :], AX)
            vsp(v)
            v.reciprocal(rdnb[:, :], dnb[:, :])
            vsp(v)
            v.tensor_scalar(wgt[:, :], esb[:, :], rdnb[:, :], 2.0,
                            ALU.mult, ALU.mult)
            vsp(v)
            for ex in range(8):
                ins = v.tensor_scalar(ghsb[:, ex * 16:(ex + 1) * 16],
                                      hsb[:, ex * 16:(ex + 1) * 16],
                                      wgt[:, ex:ex + 1], None, ALU.mult)
                if ex == 7:
                    s_gh.inc(ins)
            v.wait_ge(s_tp6.h, j + 1)
            s_ge.inc(v.tensor_copy(ghT[:, j * 128:(j + 1) * 128], ptp6[:, :]))

        @blk.vector
        def _(v):
            v.tensor_scalar(vb3a[:, :], st3b[:, :], INV_D, None, ALU.mult)
            v.tensor_scalar(mb3a[:, :], st3a[:, :], INV_D, None, ALU.mult)
            v.tensor_scalar(nme3a[:, :], st3a[:, :], -INV_D, None, ALU.mult)
            v.tensor_tensor(msq3a[:, :], mb3a[:, :], mb3a[:, :], ALU.mult)
            vsp(v)
            s_v3f.inc(v.scalar_tensor_tensor(vb3a[:, :], msq3a[:, :], -1.0,
                                             vb3a[:, :], ALU.mult, ALU.add))
            v.wait_ge(s_r3f.h, 1)
            vsp(v)
            vsp(v)
            vsp(v)
            v.tensor_tensor(nb3a[:, :], nme3a[:, :], rb3a[:, :], ALU.mult)
            vsp(v)
            for qt in range(NOT):
                v.wait_ge(s_xo.h, 16 * (qt + 1))
                v.wait_ge(s_al.h, 16 * (qt + 1))
                if qt >= 1:
                    v.wait_ge(s_n2.h, qt)
                    v.wait_ge(s_w2.h, 16 * qt)
                v.tensor_scalar(xot[:, :], xot[:, :], nb3a[:, qt:qt + 1],
                                None, ALU.add)
                s_x2r.inc(v.scalar_tensor_tensor(x2sb[:, :], atnld[qt % 2][:, :],
                                                 rb3a[:, qt:qt + 1], xot[:, :],
                                                 ALU.mult, ALU.add))
                s_s2a.inc(v.reduce_sum(sum2[:, :], x2sb[:, :], AX))
                v.wait_ge(s_s2b.h, qt + 1)
                v.tensor_scalar(vb5[:, :], sq2[:, :], INV_D, None, ALU.mult)
                v.tensor_scalar(mb5[:, :], sum2[:, :], INV_D, None, ALU.mult)
                v.tensor_scalar(nme5[:, :], sum2[:, :], -INV_D, None, ALU.mult)
                v.tensor_tensor(msq5[:, :], mb5[:, :], mb5[:, :], ALU.mult)
                vsp(v)
                s_v2f.inc(v.scalar_tensor_tensor(vb5[:, :], msq5[:, :], -1.0,
                                                 vb5[:, :], ALU.mult, ALU.add))
                v.wait_ge(s_r2f.h, qt + 1)
                s_nm2.inc(v.tensor_tensor(nb5[:, :], nme5[:, :], rb5[:, :],
                                          ALU.mult))
                base = qt * 8
                for gidx in range(8):
                    gg = base + gidx
                    v.wait_ge(s_tp5.h, gg + 1)
                    dst = x3f[:, 4 * gidx:4 * gidx + 4, qt * 128:(qt + 1) * 128]
                    s_e55.inc(v.tensor_copy(dst, ptp5[gg % 2].ap().rearrange(
                        "p (f t) -> p f t", f=4)))
                if qt >= 1:
                    top2_v(v, qt - 1)
            top2_v(v, NOT - 1)

        def top2_sc(sc, j):
            sc.wait_ge(s_t2a.h, j + 1)
            asp(sc)
            asp(sc)
            sc.activation(esb[:, :], lg[:, :], AF.Exp, bias=m1b[:, :])
            s_t2b.inc(sc.sign(s1b[:, :], d1b[:, :]))
            sc.wait_ge(s_t2c.h, j + 1)
            asp(sc)
            asp(sc)
            s_t2d.inc(sc.sign(s1b[:, :], d1b[:, :]))

        @blk.scalar
        def _(sc):
            sc.wait_ge(s_v3f.h, 1)
            asp(sc)
            asp(sc)
            asp(sc)
            sc.activation(lb3a[:, :], vb3a[:, :], AF.Ln, bias=cst[:, 2:3])
            s_r3f.inc(sc.activation(rb3a[:, :], lb3a[:, :], AF.Exp,
                                    scale=cst[:, 1:2]))
            for qt in range(NOT):
                sc.wait_ge(s_x2r.h, qt + 1)
                s_s2b.inc(sc.activation(sqj5b[:, :], x2sb[:, :], AF.Square,
                                        accum_out=sq2[:, :]))
                sc.wait_ge(s_v2f.h, qt + 1)
                asp(sc)
                asp(sc)
                sc.activation(lb5[:, :], vb5[:, :], AF.Ln, bias=cst[:, 2:3])
                s_r2f.inc(sc.activation(rb5[:, :], lb5[:, :], AF.Exp,
                                        scale=cst[:, 1:2]))
                sc.wait_ge(s_nm2.h, qt + 1)
                asp(sc)
                asp(sc)
                if qt >= 2:
                    sc.wait_ge(s_tp5.h, 8 * (qt - 1))
                s_n2.inc(sc.activation(x3t[qt % 2][:, :], x2sb[:, :], AF.Identity,
                                       bias=nb5[:, :], scale=rb5[:, :]))
                if qt >= 1:
                    top2_sc(sc, qt - 1)
            top2_sc(sc, NOT - 1)

        @blk.tensor
        def _(t):
            def rout(j):
                t.wait_ge(s_e55.h, 8 * (j + 1))
                if j >= 1:
                    t.wait_ge(s_t2a.h, j)
                last = None
                for kt in range(NKT):
                    last = t.matmul(ph6[:, :], x3f[:, kt, j * 128:(j + 1) * 128],
                                    warsb[:, kt, :],
                                    start=(kt == 0), stop=(kt == NKT - 1))
                s_p6a.inc(last)
                t.wait_ge(s_gh.h, j + 1)
                if j >= 1:
                    t.wait_ge(s_ge.h, j)
                s_tp6.inc(t.transpose(ptp6[:, :], ghsb[:, :], iden[:, :]))

            for qt in range(NOT):
                t.wait_ge(s_n2.h, qt + 1)
                for gidx in range(8):
                    gg = qt * 8 + gidx
                    if gg >= 2:
                        t.wait_ge(s_e55.h, gg - 1)
                    last = None
                    for j in range(4):
                        f = 4 * gidx + j
                        last = t.transpose(ptp5[gg % 2][:, j * 128:(j + 1) * 128],
                                           x3t[qt % 2][:, f * 128:(f + 1) * 128],
                                           iden[:, :])
                    s_tp5.inc(last)
                if qt >= 1:
                    rout(qt - 1)
            rout(NOT - 1)

    p5b.close()

    p6b = ExitStack()
    en6b = p6b.enter_context
    fstage = [en6b(nc.sbuf_tensor(f"fst{j}", [128, 256], BF16)) for j in range(2)]
    sqj6 = en6b(nc.sbuf_tensor("sqj6", [128, 256], BF16))
    st4a = en6b(nc.sbuf_tensor("st4a", [128, NOT], F32))
    st4b = en6b(nc.sbuf_tensor("st4b", [128, NOT], F32))
    psf = [en6b(nc.psum_tensor(f"psf{j}", [128, 256], F32)) for j in range(3)]

    s_p6b = sem("p6bp")
    s_ev6 = sem("p6ev")
    s_e6 = sem("p6be")
    s_q6 = sem("p6bq")
    s_fw = sem("p6fw")
    s_z6 = sem("p6z")

    with nc.Block("p6b") as blk:
        @blk.gpsimd
        def _(g):
            for oc in range(2, 16):
                g.wait_ge(s_p6b.h, 8 * (oc - 1))
                s_gwb.inc(g.dma_start(out=wbsb[oc % 2][:, :, :],
                                      in_=wb_t[oc, :, :, :]), 16)

        @blk.tensor
        def _(t):
            for oc in range(16):
                t.wait_ge(s_gwb.h, 16 * (oc + 3))
                for qt in range(NOT):
                    vidx = oc * 8 + qt
                    if vidx >= 3:
                        t.wait_ge(s_ev6.h, vidx - 2)
                    for kt in range(NKT):
                        t.matmul(psf[vidx % 3][:, :],
                                 x3f[:, kt, qt * 128:(qt + 1) * 128],
                                 wbsb[oc % 2][:, kt, :],
                                 start=(kt == 0), stop=False)
                    last = t.matmul(psf[vidx % 3][:, :],
                                    ghT[:, qt * 128:(qt + 1) * 128],
                                    bbdsb[:, oc * 256:(oc + 1) * 256],
                                    start=False, stop=True)
                    s_p6b.inc(last)

        @blk.vector
        def _(v):
            v.memset(st4a[:, :], 0.0)
            s_z6.inc(v.memset(st4b[:, :], 0.0))
            for oc in range(16):
                for qt in range(NOT):
                    vidx = oc * 8 + qt
                    v.wait_ge(s_p6b.h, vidx + 1)
                    if vidx == 0:
                        v.wait_ge(s_gwb.h, 32)
                    if vidx >= 2:
                        v.wait_ge(s_fw.h, 16 * (vidx - 1))
                        v.wait_ge(s_q6.h, vidx - 1)
                    s_ev6.inc(v.scalar_tensor_tensor(fstage[vidx % 2][:, :],
                                                     psf[vidx % 3][:, :], 1.0,
                                                     bbias[:, oc * 256:(oc + 1) * 256],
                                                     ALU.mult, ALU.add,
                                                     accum_out=pacc[vidx % 2][:, :]))
                    vsp(v)
                    v.tensor_add(st4a[:, qt:qt + 1], st4a[:, qt:qt + 1],
                                 pacc[vidx % 2][:, :])
                    v.wait_ge(s_q6.h, vidx + 1)
                    vsp(v)
                    s_e6.inc(v.tensor_add(st4b[:, qt:qt + 1], st4b[:, qt:qt + 1],
                                          qacc[vidx % 2][:, :]))

        @blk.scalar
        def _(sc):
            for oc in range(16):
                for qt in range(NOT):
                    vidx = oc * 8 + qt
                    sc.wait_ge(s_ev6.h, vidx + 1)
                    if vidx >= 2:
                        sc.wait_ge(s_e6.h, vidx - 1)
                    s_q6.inc(sc.activation(sqj6[:, :], fstage[vidx % 2][:, :],
                                           AF.Square, accum_out=qacc[vidx % 2][:, :]))

        @blk.sync
        def _(sy):
            for oc in range(16):
                for qt in range(NOT):
                    vidx = oc * 8 + qt
                    sy.wait_ge(s_ev6.h, vidx + 1)
                    s_fw.inc(sy.dma_start(
                        out=ffnd.ap()[qt, :, oc * 256:(oc + 1) * 256],
                        in_=fstage[vidx % 2][:, :]), 16)
            sy.wait_ge(s_fw.h, s_fw.n)

    p6b.close()

    p6c = ExitStack()
    en6c = p6c.enter_context
    ffnt = [en6c(nc.sbuf_tensor(f"ffnt{j}", [128, DM], BF16)) for j in range(2)]
    x2t = [en6c(nc.sbuf_tensor(f"x2t{j}", [128, DM], F32)) for j in range(2)]
    ysb = [en6c(nc.sbuf_tensor(f"ysb{j}", [128, DM], F32)) for j in range(2)]
    mb6 = en6c(nc.sbuf_tensor("mb6", [128, NOT], F32))
    vb6 = en6c(nc.sbuf_tensor("vb6", [128, NOT], F32))
    lb6 = en6c(nc.sbuf_tensor("lb6", [128, NOT], F32))
    rb6 = en6c(nc.sbuf_tensor("rb6", [128, NOT], F32))
    nb6 = en6c(nc.sbuf_tensor("nb6", [128, NOT], F32))
    msq6 = en6c(nc.sbuf_tensor("msq6", [128, NOT], F32))
    nme6 = en6c(nc.sbuf_tensor("nme6", [128, NOT], F32))

    s_ld6 = sem("p6ld")
    s_v4 = sem("p6v4")
    s_r4 = sem("p6r4")
    s_fy = sem("p6fy")
    s_yw = sem("p6yw")

    with nc.Block("p6c") as blk:
        @blk.sync
        def _(sy):
            for qt in range(min(2, NOT)):
                s_ld6.inc(sy.dma_start(out=ffnt[qt % 2][:, :], in_=ffnd.ap()[qt, :, :]), 16)
                s_ld6.inc(sy.dma_start(out=x2t[qt % 2][:, :], in_=x2d.ap()[qt, :, :]), 16)
            for qt in range(NOT):
                sy.wait_ge(s_fy.h, qt + 1)
                s_yw.inc(sy.dma_start(out=y[qt * 128:(qt + 1) * 128, :],
                                      in_=ysb[qt % 2][:, :]), 16)
                if qt + 2 < NOT:
                    s_ld6.inc(sy.dma_start(out=ffnt[qt % 2][:, :],
                                           in_=ffnd.ap()[qt + 2, :, :]), 16)
                    s_ld6.inc(sy.dma_start(out=x2t[qt % 2][:, :],
                                           in_=x2d.ap()[qt + 2, :, :]), 16)
            sy.wait_ge(s_yw.h, 16 * NOT)

        @blk.vector
        def _(v):
            v.tensor_scalar(vb6[:, :], st4b[:, :], INV_D, None, ALU.mult)
            v.tensor_scalar(mb6[:, :], st4a[:, :], INV_D, None, ALU.mult)
            v.tensor_scalar(nme6[:, :], st4a[:, :], -INV_D, None, ALU.mult)
            v.tensor_tensor(msq6[:, :], mb6[:, :], mb6[:, :], ALU.mult)
            vsp(v)
            s_v4.inc(v.scalar_tensor_tensor(vb6[:, :], msq6[:, :], -1.0,
                                            vb6[:, :], ALU.mult, ALU.add))
            v.wait_ge(s_r4.h, 1)
            vsp(v)
            vsp(v)
            vsp(v)
            v.tensor_tensor(nb6[:, :], nme6[:, :], rb6[:, :], ALU.mult)
            vsp(v)
            for qt in range(NOT):
                v.wait_ge(s_ld6.h, 32 * (qt + 1))
                if qt >= 2:
                    v.wait_ge(s_yw.h, 16 * (qt - 1))
                v.tensor_scalar(x2t[qt % 2][:, :], x2t[qt % 2][:, :],
                                nb6[:, qt:qt + 1], None, ALU.add)
                s_fy.inc(v.scalar_tensor_tensor(ysb[qt % 2][:, :], ffnt[qt % 2][:, :],
                                                rb6[:, qt:qt + 1], x2t[qt % 2][:, :],
                                                ALU.mult, ALU.add))

        @blk.scalar
        def _(sc):
            sc.wait_ge(s_v4.h, 1)
            asp(sc)
            asp(sc)
            asp(sc)
            sc.activation(lb6[:, :], vb6[:, :], AF.Ln, bias=cst[:, 2:3])
            s_r4.inc(sc.activation(rb6[:, :], lb6[:, :], AF.Exp,
                                   scale=cst[:, 1:2]))

    p6c.close()
    p5b_outer.close()
    p56.close()

    _NC_CACHE["nc"] = nc
    return nc


def _host_inputs(ins):
    x = np.ascontiguousarray(np.asarray(ins["L0_attn_hidden_states"], dtype=np.float32))
    wq = np.asarray(ins["L0_attn_wq"], dtype=np.float32)
    wk = np.asarray(ins["L0_attn_wk"], dtype=np.float32)
    wv = np.asarray(ins["L0_attn_wv"], dtype=np.float32)
    wo = np.asarray(ins["L0_attn_wo"], dtype=np.float32)
    wbase = np.asarray(ins["L0_ffn_W_base"], dtype=np.float32)
    bbase = np.asarray(ins["L0_ffn_b_base"], dtype=np.float32)
    wrout = np.asarray(ins["L0_ffn_W_router"], dtype=np.float32)
    lora_a = np.asarray(ins["L0_ffn_lora_A"], dtype=np.float32)
    lora_b = np.asarray(ins["L0_ffn_lora_B"], dtype=np.float32)

    def tile_w(w, fchunk):
        f = w.shape[1]
        return np.ascontiguousarray(
            w.reshape(NKT, 128, f // fchunk, fchunk).transpose(2, 1, 0, 3))

    wo_t = tile_w(wo, 256)
    wb_t = tile_w(np.ascontiguousarray(wbase.T), 256)
    a2t = lora_a.reshape(128, DM).T
    war = np.concatenate([a2t, wrout.T], axis=1)
    war_t = np.ascontiguousarray(war.reshape(NKT, 128, 136).transpose(1, 0, 2))
    bbd = np.ascontiguousarray(lora_b.transpose(0, 2, 1).reshape(128, DM))
    bbias = np.ascontiguousarray(np.broadcast_to(bbase[None, :], (128, DM)))

    inv = (1.0 / 10000.0 ** (np.arange(0, HD, 2, dtype=np.float64) / HD))
    pos = np.arange(S, dtype=np.float64)
    ang = pos[None, :] * np.tile(inv, 2)[:, None]
    cos = np.cos(ang)
    sin = np.sin(ang)
    sgn = np.ones((HD, 1))
    sgn[:64] = -1.0
    ssin = sin * sgn
    sc = 1.0 / np.sqrt(HD)
    rcq = (cos * sc).astype(BF)
    rsq = (ssin * sc).astype(BF)
    rck = cos.astype(BF)
    rsk = ssin.astype(BF)

    qi = np.arange(128)[:, None]
    kk = np.arange(256)[None, :]
    madd = np.zeros((128, 512), dtype=np.float32)
    madd[:, 0:256] = np.where(kk <= qi, 0.0, -200.0)
    madd[:, 256:512] = np.where(kk <= 128 + qi, 0.0, -200.0)

    iden = np.eye(128, dtype=BF)
    cst = np.zeros((128, 8), dtype=np.float32)
    cst[:, 0] = -12.0
    cst[:, 1] = -0.5
    cst[:, 2] = EPS
    cst[:, 3] = 1.0

    in_maps = []
    for c in range(8):
        b, r = c // 2, c % 2
        p_ = np.arange(128)[:, None]
        gixcols = []
        for qt in range(8):
            tokA = r * TOWN + qt * 128 + p_
            for part, w in enumerate((1536, 512, 1536, 512)):
                row = tokA + (2048 if part >= 2 else 0)
                gixcols.append(row * w)
        gix = np.concatenate(gixcols, axis=1).astype(np.int32)
        in_maps.append({
            "x_seq": x[b],
            "x_own": np.ascontiguousarray(x[b, r * TOWN:(r + 1) * TOWN]),
            "wq_t": tile_w(np.ascontiguousarray(wq[:, r * 2048:(r + 1) * 2048]), 128),
            "wk_t": tile_w(np.ascontiguousarray(wk[:, r * 2048:(r + 1) * 2048]), 128),
            "wv_t": tile_w(np.ascontiguousarray(wv[:, r * 2048:(r + 1) * 2048]), 256),
            "wo_t": wo_t, "wb_t": wb_t, "war_t": war_t,
            "bbd": bbd, "bbias": bbias,
            "rcq": rcq, "rsq": rsq, "rck": rck, "rsk": rsk,
            "madd": madd, "iden": iden, "cst": cst, "gix": gix,
        })
    return in_maps


def kernel(**inputs):
    nc = build_nc()
    in_maps = _host_inputs(inputs)
    res = run_bass_kernel_spmd(nc, in_maps, core_ids=list(range(8)))
    out = np.empty((4, S, DM), dtype=np.float32)
    for c in range(8):
        b, r = c // 2, c % 2
        out[b, r * TOWN:(r + 1) * TOWN] = np.asarray(res.results[c]["y"],
                                                     dtype=np.float32)
    return out
